# version 11 (final)
# baseline (speedup 1.0000x reference)
import contextlib
import numpy as np

import concourse.bass as bass
import concourse.mybir as mybir
from concourse.bass_utils import run_bass_kernel_spmd

F32 = mybir.dt.float32
BF16 = mybir.dt.bfloat16
I8 = mybir.dt.int8
AF = mybir.ActivationFunctionType
ALU = mybir.AluOpType
AXL = mybir.AxisListType

NCORES = 8
R = 2048
D = 2048
H = 16
HD = 128
B = 8
BL = 256
FI = 8192
KC = D // 128
EPS_LN = 1e-5
EPS_RMS = 1e-6

ENGS = ("sync", "scalar", "vector", "tensor", "gpsimd")


class SemC:
    def __init__(self, h):
        self.h = h
        self.n = 0


class Plan:

    def __init__(self, nc, sems):
        self.nc = nc
        self.sems = sems
        self.ops = {e: [] for e in ENGS}

    def wait(self, eng, sem, val):
        if val is None or val <= 0:
            return
        self.ops[eng].append(("w", sem.h, val))

    def dma(self, eng, out, in_, sem, transpose=False):
        sem.n += 16

        def fn(e, o=out, i=in_, tp=transpose):
            return e.dma_start(out=o, in_=i, transpose=tp)
        self.ops[eng].append(["i", fn, [(sem.h, 16)]])
        return sem.n

    def op(self, eng, fn, sem=None, k=1):
        incs = []
        v = None
        if sem is not None:
            sem.n += k
            v = sem.n
            incs.append((sem.h, k))
        self.ops[eng].append(["i", fn, incs])
        return v

    def mark(self, eng, sem, k=1):
        entry = None
        for e_ in reversed(self.ops[eng]):
            if e_[0] == "i":
                entry = e_
                break
        assert entry is not None, f"mark on empty stream {eng}"
        sem.n += k
        for j, (sh, kk) in enumerate(entry[2]):
            if sh is sem.h:
                entry[2][j] = (sh, kk + k)
                break
        else:
            entry[2].append((sem.h, k))
        return sem.n

    def flush(self):
        with self.nc.Block() as blk:
            for eng in ENGS:
                ops = self.ops[eng]
                if not ops:
                    continue

                def runner(e, _ops=tuple(tuple(o) for o in ops)):
                    for o in _ops:
                        if o[0] == "w":
                            e.wait_ge(o[1], o[2])
                        else:
                            inst = o[1](e)
                            for (s, k) in o[2]:
                                inst.then_inc(s, k)
                getattr(blk, eng)(runner)
        self.ops = {e: [] for e in ENGS}


def build_nc(debug=(), stop_after="full"):
    _lvl = {"p1": 1, "p2": 2, "p35": 3, "p6a": 4, "full": 5}[stop_after]
    nc = bass.Bass("TRN2", target_bir_lowering=False)
    dbg = set(debug)

    def scratch(name, shape, dtype):
        kind = "ExternalOutput" if name in dbg else "Internal"
        return nc.dram_tensor(name, shape, dtype, kind=kind).ap()

    x = nc.dram_tensor("x", [R, D], F32, kind="ExternalInput").ap()
    slope = nc.dram_tensor("slope", [H], F32, kind="ExternalInput").ap()
    wqkvu = nc.dram_tensor("wqkvu", [4 * D, D], F32, kind="ExternalInput").ap()
    wout = nc.dram_tensor("wout", [D, D], F32, kind="ExternalInput").ap()
    wg = nc.dram_tensor("wg", [D, FI], I8, kind="ExternalInput").ap()
    wu = nc.dram_tensor("wu", [D, FI], I8, kind="ExternalInput").ap()
    wd = nc.dram_tensor("wd", [FI, D], I8, kind="ExternalInput").ap()
    sg = nc.dram_tensor("sg", [FI // 128, 128], F32, kind="ExternalInput").ap()
    su = nc.dram_tensor("su", [FI // 128, 128], F32, kind="ExternalInput").ap()
    sdr = nc.dram_tensor("sdr", [1, D], F32, kind="ExternalInput").ap()
    pmask = nc.dram_tensor("pmask", [1, NCORES], F32, kind="ExternalInput").ap()
    pdist = nc.dram_tensor("pdist", [1, NCORES], F32, kind="ExternalInput").ap()
    y = nc.dram_tensor("y", [R, D], F32, kind="ExternalOutput").ap()

    xnt_scr = scratch("xnt_scr", [128, KC * R], BF16)
    wq_bf = scratch("wq_bf", [4 * D, D], BF16)
    wo_bf = scratch("wo_bf", [D, D], BF16)
    wd_bf = scratch("wd_bf", [FI, D], BF16)
    q_scr = scratch("q_scr", [B * 128, H * BL], BF16)
    k_scr = scratch("k_scr", [D, R], BF16)
    k_img = scratch("k_img", [B * 128, H * BL], BF16)
    v_scr = scratch("v_scr", [D, R], BF16)
    u_scr = scratch("u_scr", [B * 128, H * BL], F32)
    og_scr = scratch("og_scr", [B * 128, H * BL], BF16)
    fuse_scr = scratch("fuse_scr", [FI, R], BF16)
    cc_in = scratch("cc_in", [128, H * HD], F32)
    cc_out = scratch("cc_out", [NCORES * 128, H * HD], F32)
    wcb = scratch("wcb", [H, NCORES], F32)
    cc_dbg = scratch("cc_dbg", [128, H * HD], F32)
    sin_dbg = scratch("sin_dbg", [128, H * HD], F32)
    wcomb_dbg = scratch("wcomb_dbg", [128, 128], F32)

    st = contextlib.ExitStack()
    sems = {nm: SemC(st.enter_context(nc.semaphore(nm)))
            for nm in ("qd", "qs", "qg", "pe", "dv", "ac", "pl", "cc")}
    qd, qs, qg = sems["qd"], sems["qs"], sems["qg"]
    pe, dv, ac, pl, cc = (sems["pe"], sems["dv"], sems["ac"], sems["pl"],
                          sems["cc"])

    def sb(name, shape, dtype, stack=st, **kw):
        return stack.enter_context(nc.sbuf_tensor(name, shape, dtype, **kw))

    sgt = sb("sgt", [128, FI // 128], F32)
    sut = sb("sut", [128, FI // 128], F32)

    mid_st = contextlib.ExitStack()
    sbc = sb("sbc", [128, H], F32, stack=mid_st)
    nsbc = sb("nsbc", [128, H], F32, stack=mid_st)
    bdp = sb("bdp", [128, B * H], F32, stack=mid_st)
    iotam = sb("iotam", [128, BL], F32, stack=mid_st)
    idxp = sb("idxp", [128, 2 * BL], F32, stack=mid_st)
    ceps5 = sb("ceps5", [128, 1], F32, stack=mid_st)
    ceps6 = sb("ceps6", [128, 1], F32, stack=mid_st)
    on1 = sb("on1", [1, 128], F32, stack=mid_st)
    onP = sb("onP", [128, 1], BF16, stack=mid_st)
    wcomb = sb("wcomb", [128, 128], F32, stack=mid_st)
    tcol = sb("tcol", [128, 1], F32, stack=mid_st)
    tv = sb("tv", [128, 2], F32, stack=mid_st)
    otile = sb("otile", [128, HD], F32, stack=mid_st)
    Sin = sb("Sin", [128, H * HD], F32, stack=mid_st)
    Sinb = sb("Sinb", [128, H * HD], BF16, stack=mid_st)
    snap = sb("snap", [128, (B - 1) * H * HD], BF16, stack=mid_st)

    p = Plan(nc, sems)

    s_order = [4, 5, 6, 7, 8, 9, 10, 11, 0, 1, 2, 3, 12, 13, 14, 15]

    with contextlib.ExitStack() as pst:
        psq = [pst.enter_context(nc.psum_tensor(f"psq{i}", [128, 512], F32))
               for i in range(4)]
        psT = [pst.enter_context(nc.psum_tensor(f"psT{i}", [128, 1024], BF16))
               for i in range(2)]
        psS = [pst.enter_context(nc.psum_tensor(f"psS{i}", [128, 512], F32))
               for i in range(2)]
        id128 = sb("id128", [128, 128], F32, stack=pst)
        id128b = sb("id128b", [128, 128], BF16, stack=pst)
        bdvec = sb("bdvec", [128, H], F32, stack=pst)
        stv = sb("stv", [128, 2 * H], F32, stack=pst)
        xnt = sb("xnt", [128, KC * R], BF16, stack=pst)
        wts = [sb(f"wts{i}", [128, KC * 512], BF16, stack=pst,
                  is_transpose_dest=True) for i in range(2)]
        stg = [sb(f"stg{i}", [128, R], BF16, stack=pst) for i in range(2)]
        ustg = sb("ustg", [128, R], F32, stack=pst)

        p.op("gpsimd", lambda e: e.memset(ceps5[:, :], EPS_LN))
        p.op("gpsimd", lambda e: e.memset(ceps6[:, :], EPS_RMS))
        p.op("gpsimd", lambda e: e.memset(on1[:, :], 1.0))
        p.op("gpsimd", lambda e: e.memset(onP[:, :], 1.0))
        p.op("gpsimd", lambda e: e.memset(otile[:, :], 1.0))
        p.op("gpsimd", lambda e: e.memset(id128[:, :], 1.0))
        p.op("gpsimd", lambda e: e.iota(iotam[:, :], [[1, BL]], base=0,
                                        channel_multiplier=0,
                                        allow_small_or_imprecise_dtypes=True))
        p.op("gpsimd", lambda e: e.iota(tcol[:, :], [[1, 1]], base=0,
                                        channel_multiplier=1,
                                        allow_small_or_imprecise_dtypes=True))
        p.op("gpsimd", lambda e: e.affine_select(
            id128[:, :], id128[:, :], [[1, 128]], ALU.is_equal, 0.0,
            base=0, channel_multiplier=-1))
        for c in range(2):
            p.op("gpsimd", lambda e, c=c: e.iota(
                idxp[:, c * BL:(c + 1) * BL], [[1, BL]], base=-(c * 128),
                channel_multiplier=-1, allow_small_or_imprecise_dtypes=True))
            p.op("gpsimd", lambda e, c=c: e.affine_select(
                idxp[:, c * BL:(c + 1) * BL], idxp[:, c * BL:(c + 1) * BL],
                [[1, BL]], ALU.is_ge, 1.0e30,
                base=-(c * 128), channel_multiplier=-1))
        n_pl0 = p.mark("gpsimd", pl)

        n_qg_wq = {}
        for s_ in s_order[:2]:
            n_qg_wq[s_] = p.dma("gpsimd",
                                wq_bf[s_ * 512:(s_ + 1) * 512, :],
                                wqkvu[s_ * 512:(s_ + 1) * 512, :], qg)

        with contextlib.ExitStack() as p01:
            srow = sb("srow", [1, H], F32, stack=p01)
            s16 = sb("s16", [H, 1], F32, stack=p01)
            pmr = sb("pmr", [1, NCORES], F32, stack=p01)
            pdr = sb("pdr", [1, NCORES], F32, stack=p01)
            sg64 = sb("sg64", [64, 128], F32, stack=p01)
            su64 = sb("su64", [64, 128], F32, stack=p01)
            w16 = sb("w16", [H, NCORES], F32, stack=p01)
            wrow = sb("wrow", [1, H * NCORES], F32, stack=p01)
            xt = [sb(f"xt{i}", [128, D], F32, stack=p01) for i in range(3)]
            xnb = [sb(f"xnb{i}", [128, D], BF16, stack=p01) for i in range(3)]
            stat = [sb(f"stat{i}", [128, 8], F32, stack=p01) for i in range(3)]

            p.dma("sync", srow[:, :], slope.rearrange("(o h) -> o h", o=1), qd)
            p.dma("sync", s16[:, :], slope.rearrange("(h o) -> h o", o=1), qd)
            p.dma("sync", pmr[:, :], pmask, qd)
            p.dma("sync", pdr[:, :], pdist, qd)
            p.dma("sync", sg64[:, :], sg, qd)
            n_qd_consts = p.dma("sync", su64[:, :], su, qd)

            n_wslab01 = []
            for pp in range(2):
                s0 = s_order[pp]
                p.wait("sync", qg, n_qg_wq[s0])
                for k in range(KC):
                    p.dma("sync", wts[pp][:, k * 512:(k + 1) * 512],
                          wq_bf[s0 * 512:(s0 + 1) * 512,
                                k * 128:(k + 1) * 128], qd, transpose=True)
                n_wslab01.append(qd.n)

            psA, psB = psq[0], psq[1]
            p.wait("tensor", qd, n_qd_consts)
            p.wait("tensor", pl, n_pl0)
            p.op("tensor", lambda e: e.matmul(psA[:, 0:16], on1[:, :],
                                              srow[:, :], start=True, stop=True))
            p.op("tensor", lambda e: e.matmul(psA[:, 16:24], on1[:, :],
                                              pmr[:, :], start=True, stop=True))
            p.op("tensor", lambda e: e.matmul(psA[:, 24:32], on1[:, :],
                                              pdr[:, :], start=True, stop=True))
            p.op("tensor", lambda e: e.transpose(psA[:, 32:96], sg64[:, :],
                                                 id128[0:64, 0:64]))
            p.op("tensor", lambda e: e.transpose(psA[:, 96:160], su64[:, :],
                                                 id128[0:64, 0:64]))
            n_pe0 = p.mark("tensor", pe)

            p.wait("vector", pe, n_pe0)
            p.op("vector", lambda e: e.tensor_copy(sbc[:, :], psA[:, 0:16]))
            p.op("vector", lambda e: e.tensor_scalar_mul(nsbc[:, :],
                                                         psA[:, 0:16], -1.0))
            p.op("vector", lambda e: e.tensor_copy(sgt[:, :], psA[:, 32:96]))
            p.op("vector", lambda e: e.tensor_copy(sut[:, :], psA[:, 96:160]))
            p.op("vector", lambda e: e.tensor_scalar_mul(w16[:, :],
                                                         psA[0:16, 24:32],
                                                         s16[:, :]))
            p.op("vector", lambda e: e.tensor_copy(id128b[:, :], id128[:, :]))
            p.wait("vector", pl, n_pl0)
            for c in range(2):
                p.op("vector", lambda e, c=c: e.tensor_scalar_add(
                    tv[:, c:c + 1], tcol[:, :], float(128 * c - 255)))
            for c in range(2):
                for h in range(H):
                    p.op("vector", lambda e, c=c, h=h: e.tensor_mul(
                        stv[:, c * H + h:c * H + h + 1], tv[:, c:c + 1],
                        sbc[:, h:h + 1]))
            p.op("vector", lambda e: e.memset(Sin[:, :], 0.0))
            n_dv_ev0 = p.mark("vector", dv)

            p.wait("scalar", dv, n_dv_ev0)
            p.op("scalar", lambda e: e.activation(bdvec[:, :], sbc[:, :],
                                                  AF.Exp, scale=-256.0))
            for b_ in range(B):
                p.op("scalar", lambda e, b_=b_: e.activation(
                    bdp[:, b_ * H:(b_ + 1) * H], sbc[:, :], AF.Exp,
                    scale=-256.0 * b_))
            p.op("scalar", lambda e: e.activation(w16[:, :], w16[:, :],
                                                  AF.Exp, scale=-2048.0))
            n_ac_w16 = p.mark("scalar", ac)
            p.wait("vector", ac, n_ac_w16)
            p.op("vector", lambda e: e.tensor_mul(w16[:, :], w16[:, :],
                                                  psA[0:16, 16:24]))
            n_dv_w16 = p.mark("vector", dv)

            p.wait("sync", dv, n_dv_w16)
            nb1 = p.dma("sync", wcb, w16[:, :], qd)
            p.wait("sync", qd, nb1)
            nb2 = p.dma("sync", wrow[:, :],
                        wcb.rearrange("h r -> (h r)").rearrange(
                            "(o n) -> o n", o=1), qd)
            p.wait("tensor", qd, nb2)
            p.op("tensor", lambda e: e.matmul(psB[:, 0:128], on1[:, :],
                                              wrow[:, :], start=True, stop=True))
            n_pe_wc = p.mark("tensor", pe)
            p.wait("vector", pe, n_pe_wc)
            p.op("vector", lambda e: e.tensor_copy(wcomb[:, :], psB[:, 0:128]))
            n_dv_wc = p.mark("vector", dv)
            psq_init_free = [(dv, n_dv_w16), (dv, n_dv_wc), (None, 0), (None, 0)]

            xt_free = [(None, 0)] * 3
            xt_free2 = [(None, 0)] * 3
            xn_free_pe = [0, 0, 0]
            stat_free = [(None, 0)] * 3
            psT_free = [(None, 0), (None, 0)]
            for i in range(16):
                par = i % 3
                eng, qsem = "scalar", qs
                if xt_free[par][0]:
                    p.wait(eng, xt_free[par][0], xt_free[par][1])
                if xt_free2[par][0]:
                    p.wait(eng, xt_free2[par][0], xt_free2[par][1])
                n_ld = p.dma(eng, xt[par][:, :], x[i * 128:(i + 1) * 128, :],
                             qsem)
                p.wait("vector", qsem, n_ld)
                p.op("vector", lambda e, par=par: e.reduce_sum(
                    stat[par][:, 0:1], xt[par][:, :], axis=AXL.X))
                n_sum = p.mark("vector", dv)
                p.wait("scalar", qsem, n_ld)
                if stat_free[par][0]:
                    p.wait("scalar", stat_free[par][0], stat_free[par][1])
                p.op("scalar", lambda e, par=par: e.activation(
                    ustg[:, :], xt[par][:, :], AF.Square,
                    accum_out=stat[par][:, 2:3]))
                p.wait("scalar", dv, n_sum)
                p.op("scalar", lambda e, par=par: e.mul(
                    stat[par][:, 1:2], stat[par][:, 0:1], 1.0 / D))
                p.op("scalar", lambda e, par=par: e.activation(
                    stat[par][:, 3:4], stat[par][:, 0:1], AF.Square))
                n_ss = p.mark("scalar", ac)
                p.wait("vector", ac, n_ss)
                p.op("vector", lambda e, par=par: e.scalar_tensor_tensor(
                    stat[par][:, 4:5], stat[par][:, 3:4], -1.0 / D,
                    stat[par][:, 2:3], ALU.mult, ALU.add))
                n_var = p.mark("vector", dv)
                p.wait("scalar", dv, n_var)
                p.op("scalar", lambda e, par=par: e.activation(
                    stat[par][:, 5:6], stat[par][:, 4:5], AF.Ln, scale=1.0 / D,
                    bias=ceps5[:, :]))
                p.op("scalar", lambda e, par=par: e.activation(
                    stat[par][:, 6:7], stat[par][:, 5:6], AF.Exp, scale=-0.5))
                n_rstd = p.mark("scalar", ac)
                p.wait("vector", ac, n_rstd)
                p.wait("vector", pe, xn_free_pe[par])
                p.op("vector", lambda e, par=par: e.tensor_scalar(
                    xnb[par][:, :], xt[par][:, :], stat[par][:, 1:2],
                    stat[par][:, 6:7], ALU.subtract, ALU.mult))
                n_xn = p.mark("vector", dv)
                xt_free[par] = (dv, n_xn)
                xt_free2[par] = (ac, n_ss)
                stat_free[par] = (dv, n_xn)
                p.wait("tensor", dv, n_xn)
                for half in range(2):
                    tb = half
                    if psT_free[tb][0]:
                        p.wait("tensor", psT_free[tb][0], psT_free[tb][1])
                    for k in range(8):
                        kk = half * 8 + k
                        p.op("tensor", lambda e, par=par, tb=tb, k=k, kk=kk:
                             e.transpose(psT[tb][:, k * 128:(k + 1) * 128],
                                         xnb[par][:, kk * 128:(kk + 1) * 128],
                                         id128b[:, :]))
                    n_tp = p.mark("tensor", pe)
                    p.wait("vector", pe, n_tp)
                    for k in range(8):
                        kk = half * 8 + k
                        p.op("vector", lambda e, tb=tb, k=k, kk=kk, i=i:
                             e.tensor_copy(
                                 xnt[:, kk * R + i * 128:kk * R + i * 128 + 128],
                                 psT[tb][:, k * 128:(k + 1) * 128]))
                    n_ev = p.mark("vector", dv)
                    psT_free[tb] = (dv, n_ev)
                xn_free_pe[par] = pe.n
            n_dv_xnt = dv.n

        ktok = [sb(f"ktok{i}", [128, 2 * D], BF16, stack=pst,
                   is_transpose_dest=True) for i in range(1)]
        vtok3 = [sb(f"vtok3{i}", [128, 2 * D], BF16, stack=pst,
                    is_transpose_dest=True) for i in range(1)]
        kd_bc = [sb(f"kdbc{c}", [128, D], BF16, stack=pst) for c in range(2)]
        S = sb("S", [128, H * HD], F32, stack=pst)

        for c in range(2):
            for h in range(H):
                p.op("scalar", lambda e, c=c, h=h: e.activation(
                    kd_bc[c][:, h * HD:(h + 1) * HD], otile[:, :], AF.Exp,
                    scale=stv[:, c * H + h:c * H + h + 1]))
        n_ac_kd = p.mark("scalar", ac)
        p.wait("vector", ac, n_ac_kd)
        p.op("vector", lambda e: e.memset(S[:, :], 0.0))
        n_dv_s0 = p.mark("vector", dv)

        wts_free = [0, 0]
        stg_free = [None, None]
        ustg_free = None
        psum_free = list(psq_init_free)
        slot_i = 0
        n_qs_kdone = None
        n_qs_vdone = None
        p3_tok_free = 0
        p3_ps_free = [(None, 0), (None, 0)]
        p3_upd_prev = n_dv_s0
        p3_n_tok = 0
        n_qd_cc = None

        def emit_p3_block(b_):
            nonlocal_vals = None

        p3_state = {"tok_free": [0], "upd_prev": n_dv_s0,
                    "ps_free": [(None, 0), (None, 0)]}

        def p3_block(b_):
            st3 = p3_state
            bp = 0
            kt, vt = ktok[bp], vtok3[bp]
            p.wait("sync", pe, st3["tok_free"][bp])
            p.wait("sync", qs, n_qs_vdone)
            for c in range(2):
                p.dma("sync", kt[:, c * D:(c + 1) * D],
                      k_scr[:, b_ * BL + c * 128:b_ * BL + (c + 1) * 128],
                      qd, transpose=True)
                p.dma("sync", vt[:, c * D:(c + 1) * D],
                      v_scr[:, b_ * BL + c * 128:b_ * BL + (c + 1) * 128],
                      qd, transpose=True)
            n_tok = qd.n
            p.wait("vector", qd, n_tok)
            p.wait("vector", pe, st3["tok_free"][bp])
            for c in range(2):
                p.op("vector", lambda e, c=c, vt=vt: e.tensor_mul(
                    vt[:, c * D:(c + 1) * D], vt[:, c * D:(c + 1) * D],
                    kd_bc[c][:, :]))
            if b_ >= 1:
                p.op("vector", lambda e, b_=b_: e.tensor_copy(
                    snap[:, (b_ - 1) * H * HD:b_ * H * HD], S[:, :]))
            n_ksc = p.mark("vector", dv)
            for h in range(H):
                hp = h % 2
                p.wait("tensor", dv, n_ksc)
                if st3["ps_free"][hp][0]:
                    p.wait("tensor", st3["ps_free"][hp][0],
                           st3["ps_free"][hp][1])
                for c in range(2):
                    p.op("tensor", lambda e, hp=hp, c=c, h=h, kt=kt, vt=vt:
                         e.matmul(psS[hp][:, 0:HD],
                                  kt[:, c * D + h * HD:c * D + (h + 1) * HD],
                                  vt[:, c * D + h * HD:c * D + (h + 1) * HD],
                                  start=(c == 0), stop=(c == 1)))
                n_mm = p.mark("tensor", pe)
                p.wait("vector", pe, n_mm)
                p.op("vector", lambda e, hp=hp, h=h: e.scalar_tensor_tensor(
                    S[:, h * HD:(h + 1) * HD], S[:, h * HD:(h + 1) * HD],
                    bdvec[:, h:h + 1], psS[hp][:, 0:HD], ALU.mult, ALU.add))
                n_upd = p.mark("vector", dv)
                st3["ps_free"][hp] = (dv, n_upd)
                st3["upd_prev"] = n_upd
            st3["tok_free"][bp] = pe.n

        n_spill_xnt = 0
        wts_loaded = {0: n_wslab01[0], 1: n_wslab01[1]}

        def emit_wts_loads(pp):
            s2_ = s_order[pp]
            par_ = pp % 2
            p.wait("sync", qg, n_qg_wq[s2_])
            p.wait("sync", pe, wts_free[par_])
            for k in range(KC):
                p.dma("sync", wts[par_][:, k * 512:(k + 1) * 512],
                      wq_bf[s2_ * 512:(s2_ + 1) * 512,
                            k * 128:(k + 1) * 128],
                      qd, transpose=True)
            wts_loaded[pp] = qd.n

        for pos, s_ in enumerate(s_order):
            par = pos % 2
            kind = ("q", "k", "v", "u")[s_ // 4]
            chunk0 = (s_ % 4) * 4
            n_wslab = wts_loaded[pos]
            p.wait("tensor", qd, n_wslab)
            p.wait("tensor", dv, n_dv_xnt)
            for fc in range(4):
                chunk = chunk0 + fc
                ev_marks = []
                for t in range(4):
                    slot = slot_i % 4
                    slot_i += 1
                    if psum_free[slot][0]:
                        p.wait("tensor", psum_free[slot][0],
                               psum_free[slot][1])
                    for k in range(KC):
                        p.op("tensor",
                             lambda e, slot=slot, par=par, fc=fc, t=t, k=k:
                             e.matmul(psq[slot][:, :],
                                      wts[par][:, k * 512 + fc * 128:
                                               k * 512 + fc * 128 + 128],
                                      xnt[:, k * R + t * 512:
                                          k * R + (t + 1) * 512],
                                      start=(k == 0), stop=(k == KC - 1)))
                    n_mm = p.mark("tensor", pe)
                    if kind in ("q", "k"):
                        p.wait("scalar", pe, n_mm)
                        if t == 0 and stg_free[par]:
                            p.wait("scalar", qs, stg_free[par][1])
                        p.op("scalar",
                             lambda e, slot=slot, par=par, t=t: e.activation(
                                 stg[par][:, t * 512:(t + 1) * 512],
                                 psq[slot][:, :], AF.Silu))
                        nev = p.mark("scalar", ac)
                        psum_free[slot] = (ac, nev)
                    else:
                        p.wait("vector", pe, n_mm)
                        if t == 0:
                            fr = stg_free[par] if kind == "v" else ustg_free
                            if fr:
                                p.wait("vector", qs, fr[1])
                        if kind == "v":
                            p.op("vector",
                                 lambda e, slot=slot, par=par, t=t:
                                 e.tensor_copy(
                                     stg[par][:, t * 512:(t + 1) * 512],
                                     psq[slot][:, :]))
                        else:
                            p.op("vector",
                                 lambda e, slot=slot, t=t:
                                 e.tensor_copy(
                                     ustg[:, t * 512:(t + 1) * 512],
                                     psq[slot][:, :]))
                        nev = p.mark("vector", dv)
                        psum_free[slot] = (dv, nev)
                    ev_marks.append(psum_free[slot])
                last = ev_marks[-1]
                p.wait("scalar", last[0], last[1])
                if kind == "q":
                    for b_ in range(B):
                        p.dma("scalar",
                              q_scr[b_ * 128:(b_ + 1) * 128,
                                    chunk * BL:(chunk + 1) * BL],
                              stg[par][:, b_ * BL:(b_ + 1) * BL], qs)
                    stg_free[par] = ("qs", qs.n)
                elif kind == "k":
                    p.dma("scalar", k_scr[chunk * 128:(chunk + 1) * 128, :],
                          stg[par][:, :], qs)
                    for b_ in range(B):
                        p.dma("scalar",
                              k_img[b_ * 128:(b_ + 1) * 128,
                                    chunk * BL:(chunk + 1) * BL],
                              stg[par][:, b_ * BL:(b_ + 1) * BL], qs)
                    stg_free[par] = ("qs", qs.n)
                elif kind == "v":
                    p.dma("scalar", v_scr[chunk * 128:(chunk + 1) * 128, :],
                          stg[par][:, :], qs)
                    stg_free[par] = ("qs", qs.n)
                else:
                    for b_ in range(B):
                        p.dma("scalar",
                              u_scr[b_ * 128:(b_ + 1) * 128,
                                    chunk * BL:(chunk + 1) * BL],
                              ustg[:, b_ * BL:(b_ + 1) * BL], qs)
                    ustg_free = ("qs", qs.n)
            wts_free[par] = pe.n
            if 2 <= pos + 1 < 16:
                emit_wts_loads(pos + 1)
            if pos + 2 < 16:
                s2 = s_order[pos + 2]
                p.wait("gpsimd", qd, n_wslab)
                n_qg_wq[s2] = p.dma("gpsimd",
                                    wq_bf[s2 * 512:(s2 + 1) * 512, :],
                                    wqkvu[s2 * 512:(s2 + 1) * 512, :], qg)
            if pos == 3:
                n_qs_kdone = qs.n
            if pos == 7:
                n_qs_vdone = qs.n
            if pos == 8:
                p.wait("sync", dv, n_dv_xnt)
                for j in range(4):
                    p.dma("sync", xnt_scr[:, j * 8192:(j + 1) * 8192],
                          xnt[:, j * 8192:(j + 1) * 8192], qd)
                n_spill_xnt = qd.n
            if 8 <= pos <= 11:
                p3_block(2 * (pos - 8))
                p3_block(2 * (pos - 8) + 1)
            if pos == 11:
                p.wait("sync", dv, p3_state["upd_prev"])
                p.dma("sync", cc_in, S[:, :], qd)
                n_qd_cc = qd.n
                if "cc_dbg" in dbg:
                    p.dma("sync", cc_dbg, S[:, :], qd)
                    p.wait("sync", qd, qd.n)
                p.wait("gpsimd", qd, n_qd_cc)
                p.op("gpsimd", lambda e: e.collective_compute(
                    "AllGather", ALU.bypass,
                    replica_groups=[list(range(NCORES))],
                    ins=[cc_in], outs=[cc_out]), cc)
        n_cc = cc.n

        gr4 = [sb(f"gr4{i}", [128, H * HD], F32, stack=pst) for i in range(1)]
        Sin2 = sb("Sin2", [128, H * HD], F32, stack=pst)
        p.op("vector", lambda e: e.memset(Sin2[:, :], 0.0))
        p.mark("vector", dv)
        p.wait("gpsimd", cc, n_cc)
        gr_free = [0]
        for r in range(NCORES):
            par4 = 0
            p.wait("gpsimd", dv, gr_free[par4])
            n_ld = p.dma("gpsimd", gr4[par4][:, :],
                         cc_out[r * 128:(r + 1) * 128, :], qg)
            src_ = Sin if r % 2 == 0 else Sin2
            dst = Sin2 if r % 2 == 0 else Sin
            p.wait("vector", qg, n_ld)
            for h in range(H):
                p.op("vector",
                     lambda e, par4=par4, r=r, h=h, s=src_, d=dst:
                     e.scalar_tensor_tensor(
                         d[:, h * HD:(h + 1) * HD],
                         gr4[par4][:, h * HD:(h + 1) * HD],
                         wcomb[:, h * NCORES + r:h * NCORES + r + 1],
                         s[:, h * HD:(h + 1) * HD], ALU.mult, ALU.add))
            gr_free[par4] = p.mark("vector", dv)
        p.op("vector", lambda e: e.tensor_copy(Sinb[:, :], Sin[:, :]))
        n_dv_sin = p.mark("vector", dv)
        if "sin_dbg" in dbg:
            p.wait("sync", dv, n_dv_sin)
            p.dma("sync", sin_dbg, Sin[:, :], qd)
            p.dma("sync", wcomb_dbg, wcomb[:, :], qd)

        n_qd_p2 = qd.n
        n_qs_p2 = qs.n
        p.flush()

    if _lvl >= 3:
        with contextlib.ExitStack() as pst:
            qdt = sb("qdt", [128, H * BL], BF16, stack=pst)
            mskt = sb("mskt", [128, 2 * H * BL], BF16, stack=pst)

            p.wait("scalar", dv, n_dv_sin)
            p.wait("sync", dv, n_dv_sin)

            p.wait("scalar", pl, n_pl0)
            for h in range(H):
                p.op("scalar", lambda e, h=h: e.activation(
                    qdt[:, h * BL:(h + 1) * BL], iotam[:, :], AF.Exp,
                    scale=nsbc[:, h:h + 1], bias=nsbc[:, h:h + 1]))
            for h in range(H):
                for c in range(2):
                    p.op("scalar", lambda e, c=c, h=h: e.activation(
                        mskt[:, (h * 2 + c) * BL:(h * 2 + c + 1) * BL],
                        idxp[:, c * BL:(c + 1) * BL], AF.Exp,
                        scale=nsbc[:, h:h + 1]))
            n_ac_tab = p.mark("scalar", ac)
            p.wait("sync", qs, n_qs_p2)
            p.wait("scalar", qs, n_qs_p2)

            with contextlib.ExitStack() as p5st:
                psO = [p5st.enter_context(
                    nc.psum_tensor(f"psO{i}", [128, 512], F32))
                    for i in range(3)]
                psQK = [p5st.enter_context(
                    nc.psum_tensor(f"psQK{i}", [128, 512], F32))
                    for i in range(3)]
                psSS = p5st.enter_context(
                    nc.psum_tensor("psSS", [128, 512], F32))
                psR = p5st.enter_context(
                    nc.psum_tensor("psR", [128, 512], F32))
                qT = [sb(f"qT{i}", [128, H * BL], BF16, stack=p5st)
                      for i in range(2)]
                kT = [sb(f"kT{i}", [128, H * BL], BF16, stack=p5st)
                      for i in range(2)]
                vtok = [sb(f"vtok{i}", [128, 2 * D], BF16, stack=p5st,
                           is_transpose_dest=True) for i in range(2)]
                ublk = sb("ublk", [128, H * BL], F32, stack=p5st)
                ob = sb("ob", [128, H * BL], F32, stack=p5st)
                sqog = sb("sqog", [128, H * BL], BF16, stack=p5st)
                qtl = [sb(f"qtl{i}", [128, H * BL], BF16, stack=p5st)
                       for i in range(2)]
                gall = [sb(f"gall{i}", [128, H * HD], BF16, stack=p5st)
                        for i in range(2)]
                bdpS = sb("bdpS", [128, H * HD], BF16, stack=p5st)
                qkm = [sb(f"qkm{i}", [128, 2 * BL], BF16, stack=p5st)
                       for i in range(4)]
                rstd5 = sb("rstd5", [1, BL], F32, stack=p5st)

                p.wait("vector", ac, n_ac_tab)

                blk_free_pe = [0, 0]
                blk_free_dv = [0, 0]
                u_free_dv = 0
                ob_free_dv = 0
                sqog_free_qd = 0
                psO_free = [(None, 0)] * 3
                psQK_free = [(None, 0)] * 3
                qtl_free = [0, 0]
                gall_free = [0, 0]
                bdpS_free = 0
                qkm_free = [0, 0, 0, 0]
                psSS_free = (None, 0)
                psR_free = (None, 0)
                last_ssmm = 0

                for b_ in range(B):
                    par = b_ % 2
                    p.wait("sync", pe, blk_free_pe[par])
                    p.wait("sync", dv, blk_free_dv[par])
                    p.dma("sync", kT[par][:, :],
                          k_img[b_ * 128:(b_ + 1) * 128, :], qd)
                    for c in range(2):
                        p.dma("sync", vtok[par][:, c * D:(c + 1) * D],
                              v_scr[:, b_ * BL + c * 128:
                                    b_ * BL + (c + 1) * 128],
                              qd, transpose=True)
                    n_tok = qd.n
                    p.wait("scalar", pe, blk_free_pe[par])
                    p.wait("scalar", dv, blk_free_dv[par])
                    n_q = p.dma("scalar", qT[par][:, :],
                                q_scr[b_ * 128:(b_ + 1) * 128, :], qs)
                    p.wait("scalar", dv, u_free_dv)
                    n_u = p.dma("scalar", ublk[:, :],
                                u_scr[b_ * 128:(b_ + 1) * 128, :], qs)

                    p.wait("vector", qs, n_q)
                    p.wait("vector", pe, qtl_free[par])
                    p.op("vector", lambda e, par=par: e.tensor_mul(
                        qtl[par][:, :], qT[par][:, :], qdt[:, :]))
                    n_qtl = p.mark("vector", dv)
                    if b_ >= 1:
                        p.wait("scalar", dv, bdpS_free)
                        for h in range(H):
                            p.op("scalar", lambda e, h=h, b_=b_:
                                 e.activation(
                                     bdpS[:, h * HD:(h + 1) * HD],
                                     Sin[:, h * HD:(h + 1) * HD], AF.Copy,
                                     scale=bdp[:, b_ * H + h:b_ * H + h + 1]))
                        n_bs = p.mark("scalar", ac)
                        p.wait("vector", ac, n_bs)
                        p.wait("vector", pe, gall_free[par])
                        p.op("vector", lambda e, par=par, b_=b_: e.tensor_add(
                            gall[par][:, :], bdpS[:, :],
                            snap[:, (b_ - 1) * H * HD:b_ * H * HD]))
                        n_gall = p.mark("vector", dv)
                        bdpS_free = n_gall
                    else:
                        n_gall = 0

                    ev_marks = []
                    for pr in range(H // 2):
                        hh = (2 * pr, 2 * pr + 1)
                        sl = ((2 * pr) % 3, (2 * pr + 1) % 3)
                        p.wait("tensor", dv, max(n_gall, n_qtl))
                        p.wait("tensor", qd, n_tok)
                        p.wait("tensor", qs, n_q)
                        for j in range(2):
                            h, hp = hh[j], sl[j]
                            if psO_free[hp][0]:
                                p.wait("tensor", psO_free[hp][0],
                                       psO_free[hp][1])
                            glhs = (gall[par] if b_ >= 1 else Sinb)
                            p.op("tensor", lambda e, hp=hp, par=par, h=h,
                                 g=glhs:
                                 e.matmul(psO[hp][:, 0:BL],
                                          g[:, h * HD:(h + 1) * HD],
                                          qtl[par][:, h * BL:(h + 1) * BL],
                                          start=True, stop=False))
                            if psQK_free[hp][0]:
                                p.wait("tensor", psQK_free[hp][0],
                                       psQK_free[hp][1])
                            for c in range(2):
                                p.op("tensor",
                                     lambda e, par=par, hp=hp, c=c, h=h:
                                     e.matmul(
                                         psQK[hp][:, c * BL:(c + 1) * BL],
                                         kT[par][:, h * BL + c * 128:
                                                 h * BL + (c + 1) * 128],
                                         qT[par][:, h * BL:(h + 1) * BL],
                                         start=True, stop=True))
                        n_qk = p.mark("tensor", pe)
                        qtl_free[par] = n_qk
                        gall_free[par] = n_qk
                        p.wait("vector", pe, n_qk)
                        for j in range(2):
                            h, hp = hh[j], sl[j]
                            qi = (pr % 2) * 2 + j
                            p.wait("vector", pe, qkm_free[qi])
                            p.op("vector", lambda e, hp=hp, qi=qi, h=h:
                                 e.tensor_mul(
                                     qkm[qi][:, :], psQK[hp][:, :],
                                     mskt[:, h * 2 * BL:(h + 1) * 2 * BL]))
                        n_qkm = p.mark("vector", dv)
                        for j in range(2):
                            psQK_free[sl[j]] = (dv, n_qkm)
                        p.wait("tensor", dv, n_qkm)
                        for j in range(2):
                            h, hp = hh[j], sl[j]
                            qi = (pr % 2) * 2 + j
                            for c in range(2):
                                p.op("tensor",
                                     lambda e, par=par, hp=hp, qi=qi, c=c, h=h:
                                     e.matmul(psO[hp][:, 0:BL],
                                              vtok[par][:, c * D + h * HD:
                                                        c * D + (h + 1) * HD],
                                              qkm[qi][:, c * BL:(c + 1) * BL],
                                              start=False, stop=(c == 1)))
                        n_diag = p.mark("tensor", pe)
                        qkm_free[(pr % 2) * 2] = n_diag
                        qkm_free[(pr % 2) * 2 + 1] = n_diag
                        p.wait("scalar", pe, n_diag)
                        if pr == 0:
                            p.wait("scalar", dv, ob_free_dv)
                        for j in range(2):
                            h, hp = hh[j], sl[j]
                            p.op("scalar", lambda e, hp=hp, h=h: e.copy(
                                ob[:, h * BL:(h + 1) * BL], psO[hp][:, 0:BL]))
                        nev = p.mark("scalar", ac)
                        for j in range(2):
                            psO_free[sl[j]] = (ac, nev)
                        ev_marks.append(nev)
                    blk_free_pe[par] = pe.n
                    blk_free_dv[par] = dv.n

                    n_all_ev = ev_marks[-1]
                    p.wait("scalar", pe, last_ssmm)
                    p.wait("scalar", qd, sqog_free_qd)
                    p.op("scalar", lambda e: e.activation(
                        sqog[:, :], ob[:, :], AF.Square))
                    n_sq = p.mark("scalar", ac)
                    p.wait("tensor", ac, n_sq)
                    if psSS_free[0]:
                        p.wait("tensor", psSS_free[0], psSS_free[1])
                    for h in range(H):
                        p.op("tensor", lambda e, h=h: e.matmul(
                            psSS[0:1, 0:BL], onP[:, :],
                            sqog[:, h * BL:(h + 1) * BL],
                            start=(h == 0), stop=(h == H - 1)))
                    n_ssmm = p.mark("tensor", pe)
                    last_ssmm = n_ssmm
                    p.wait("scalar", pe, n_ssmm)
                    p.op("scalar", lambda e: e.activation(
                        rstd5[:, :], psSS[0:1, 0:BL], AF.Ln, scale=1.0 / D,
                        bias=ceps6[0:1, :]))
                    p.op("scalar", lambda e: e.activation(
                        rstd5[:, :], rstd5[:, :], AF.Exp, scale=-0.5))
                    n_rstd = p.mark("scalar", ac)
                    psSS_free = (ac, n_rstd)
                    p.wait("tensor", ac, n_rstd)
                    if psR_free[0]:
                        p.wait("tensor", psR_free[0], psR_free[1])
                    p.op("tensor", lambda e: e.matmul(
                        psR[:, 0:BL], on1[:, :], rstd5[:, :], start=True,
                        stop=True))
                    n_rbc = p.mark("tensor", pe)
                    p.wait("vector", pe, n_rbc)
                    p.wait("vector", qs, n_u)
                    p.wait("vector", ac, n_all_ev)
                    p.op("vector", lambda e: e.tensor_mul(
                        ob[:, :], ob[:, :], ublk[:, :]))
                    for h in range(H):
                        p.op("vector", lambda e, h=h: e.tensor_mul(
                            sqog[:, h * BL:(h + 1) * BL],
                            ob[:, h * BL:(h + 1) * BL], psR[:, 0:BL]))
                    n_gat = p.mark("vector", dv)
                    psR_free = (dv, n_gat)
                    ob_free_dv = n_gat
                    u_free_dv = n_gat
                    p.wait("sync", dv, n_gat)
                    p.dma("sync", og_scr[b_ * 128:(b_ + 1) * 128, :],
                          sqog[:, :], qd)
                    sqog_free_qd = qd.n
                    p.wait("gpsimd", qd, sqog_free_qd)
                    p.dma("gpsimd", wd_bf[b_ * 1024:(b_ + 1) * 1024, :],
                          wd[b_ * 1024:(b_ + 1) * 1024, :], qg)
                    if b_ >= 4:
                        s_ = b_ - 4
                        p.dma("gpsimd", wo_bf[s_ * 512:(s_ + 1) * 512, :],
                              wout[s_ * 512:(s_ + 1) * 512, :], qg)
                n_qg_wd = qg.n
                n_qg_wo = qg.n
                n_qd_p5 = qd.n
                n_dv_p5 = dv.n
                n_ac_p5 = ac.n
                n_pe_p5 = pe.n

                p.flush()

    mid_st.close()

    if _lvl >= 4:
        with contextlib.ExitStack() as pst:
            psG = [pst.enter_context(nc.psum_tensor(f"psG{i}", [128, 512], F32))
                   for i in range(2)]
            psU = [pst.enter_context(nc.psum_tensor(f"psU{i}", [128, 512], F32))
                   for i in range(2)]
            xnt2 = sb("xnt2", [128, KC * R], BF16, stack=pst)
            wgs = [sb(f"wgs{i}", [128, KC * 512], BF16, stack=pst)
                   for i in range(2)]
            wus = [sb(f"wus{i}", [128, KC * 512], BF16, stack=pst)
                   for i in range(2)]
            gtmp = [sb(f"gtmp{i}", [128, 512], F32, stack=pst) for i in range(2)]
            utmp = [sb(f"utmp{i}", [128, 512], F32, stack=pst) for i in range(2)]
            fstg = [sb(f"fstg{i}", [128, R], BF16, stack=pst) for i in range(2)]

            p.wait("gpsimd", qd, n_qd_p5)
            p.wait("gpsimd", dv, n_dv_p5)
            p.wait("gpsimd", ac, n_ac_p5)
            p.wait("gpsimd", pe, n_pe_p5)

            p.wait("sync", qd, n_spill_xnt)
            for j in range(4):
                p.dma("sync", xnt2[:, j * 8192:(j + 1) * 8192],
                      xnt_scr[:, j * 8192:(j + 1) * 8192], qd)
            n_xnt2 = qd.n

            slab_free = [0, 0]
            fstg_free = [0, 0]
            psG_free = [(None, 0), (None, 0)]
            psU_free = [(None, 0), (None, 0)]
            gtmp_free = [0, 0]
            utmp_free = [0, 0]
            for fg in range(16):
                par = fg % 2
                p.wait("gpsimd", pe, slab_free[par])
                for k in range(KC):
                    p.dma("gpsimd", wgs[par][:, k * 512:(k + 1) * 512],
                          wg[k * 128:(k + 1) * 128, fg * 512:(fg + 1) * 512], qg)
                for k in range(KC):
                    p.dma("gpsimd", wus[par][:, k * 512:(k + 1) * 512],
                          wu[k * 128:(k + 1) * 128, fg * 512:(fg + 1) * 512], qg)
                n_slab = qg.n
                p.wait("tensor", qg, n_slab)
                p.wait("tensor", qd, n_xnt2)
                for fc in range(4):
                    fstg_par = fc % 2
                    for t in range(4):
                        sp = (fc * 4 + t) % 2
                        if psG_free[sp][0]:
                            p.wait("tensor", psG_free[sp][0], psG_free[sp][1])
                        for k in range(KC):
                            p.op("tensor",
                                 lambda e, sp=sp, par=par, fc=fc, t=t, k=k:
                                 e.matmul(psG[sp][:, :],
                                          wgs[par][:, k * 512 + fc * 128:
                                                   k * 512 + fc * 128 + 128],
                                          xnt2[:, k * R + t * 512:
                                               k * R + (t + 1) * 512],
                                          start=(k == 0), stop=(k == KC - 1)))
                        n_g = p.mark("tensor", pe)
                        if psU_free[sp][0]:
                            p.wait("tensor", psU_free[sp][0], psU_free[sp][1])
                        for k in range(KC):
                            p.op("tensor",
                                 lambda e, sp=sp, par=par, fc=fc, t=t, k=k:
                                 e.matmul(psU[sp][:, :],
                                          wus[par][:, k * 512 + fc * 128:
                                                   k * 512 + fc * 128 + 128],
                                          xnt2[:, k * R + t * 512:
                                               k * R + (t + 1) * 512],
                                          start=(k == 0), stop=(k == KC - 1)))
                        n_uu = p.mark("tensor", pe)
                        p.wait("scalar", pe, n_uu)
                        p.wait("scalar", dv, gtmp_free[sp])
                        p.op("scalar", lambda e, sp=sp, fg=fg, fc=fc: e.activation(
                            gtmp[sp][:, :], psG[sp][:, :], AF.Silu,
                            scale=sgt[:, fg * 4 + fc:fg * 4 + fc + 1]))
                        p.op("scalar", lambda e, sp=sp, fg=fg, fc=fc: e.activation(
                            utmp[sp][:, :], psU[sp][:, :], AF.Copy,
                            scale=sut[:, fg * 4 + fc:fg * 4 + fc + 1]))
                        n_ev = p.mark("scalar", ac)
                        psG_free[sp] = (ac, n_ev)
                        psU_free[sp] = (ac, n_ev)
                        p.wait("vector", ac, n_ev)
                        if t == 0:
                            p.wait("vector", qd, fstg_free[fstg_par])
                        p.op("vector", lambda e, sp=sp, fstg_par=fstg_par, t=t:
                             e.tensor_mul(fstg[fstg_par][:, t * 512:(t + 1) * 512],
                                          gtmp[sp][:, :], utmp[sp][:, :]))
                        n_f = p.mark("vector", dv)
                        gtmp_free[sp] = n_f
                        utmp_free[sp] = n_f
                    p.wait("sync", dv, n_f)
                    p.dma("sync", fuse_scr[(fg * 4 + fc) * 128:
                                           (fg * 4 + fc + 1) * 128, :],
                          fstg[fstg_par][:, :], qd)
                    fstg_free[fstg_par] = qd.n
                slab_free[par] = pe.n
            n_qd_p6a = qd.n
            n_pe_p6a = pe.n
            n_dv_p6a = dv.n
            n_ac_p6a = ac.n
            p.flush()

    if _lvl >= 5:
        with contextlib.ExitStack() as pst:
            psY = [pst.enter_context(nc.psum_tensor(f"psY{i}", [128, 512], F32))
                   for i in range(8)]
            fuT = sb("fuT", [128, 64 * 512], BF16, stack=pst)
            ogT = sb("ogT", [128, KC * 512], BF16, stack=pst)
            woT = sb("woT", [128, KC * 2048], BF16, stack=pst,
                     is_transpose_dest=True)
            wds = [sb(f"wds{i}", [128, 1024], BF16, stack=pst) for i in range(4)]
            xts = [sb(f"xts{i}", [128, 1024], F32, stack=pst) for i in range(4)]
            ystg = [sb(f"ystg{i}", [128, 512], F32, stack=pst) for i in range(2)]
            onb = sb("onb", [1, 128], F32, stack=pst)
            sd6 = sb("sd6", [1, D], F32, stack=pst)
            sdbb = sb("sdbb", [128, D], F32, stack=pst)
            rsbb = sb("rsbb", [128, D], BF16, stack=pst)

            psY_free = [(None, 0)] * 8

            p.wait("gpsimd", pe, n_pe_p6a)
            p.wait("gpsimd", dv, n_dv_p6a)
            p.wait("gpsimd", ac, n_ac_p6a)
            p.wait("gpsimd", qd, n_qd_p6a)

            p.op("gpsimd", lambda e: e.memset(onb[:, :], 1.0))
            n_pl6 = p.mark("gpsimd", pl)
            n_sd = p.dma("sync", sd6[:, :], sdr, qd)
            p.wait("tensor", pl, n_pl6)
            p.wait("tensor", qd, n_sd)
            prev = 0
            for j in range(4):
                p.wait("tensor", dv, prev)
                p.op("tensor", lambda e, j=j: e.matmul(
                    psY[0][:, :], onb[:, :], sd6[:, j * 512:(j + 1) * 512],
                    start=True, stop=True))
                nmm = p.mark("tensor", pe)
                p.wait("vector", pe, nmm)
                p.op("vector", lambda e, j=j: e.tensor_copy(
                    sdbb[:, j * 512:(j + 1) * 512], psY[0][:, :]))
                prev = p.mark("vector", dv)
            p.op("vector", lambda e: e.reciprocal(sd6[:, :], sd6[:, :]))
            n_dv_rs = p.mark("vector", dv)
            p.wait("tensor", dv, n_dv_rs)
            for j in range(4):
                p.wait("tensor", dv, prev)
                p.op("tensor", lambda e, j=j: e.matmul(
                    psY[0][:, :], onb[:, :], sd6[:, j * 512:(j + 1) * 512],
                    start=True, stop=True))
                nmm = p.mark("tensor", pe)
                p.wait("vector", pe, nmm)
                p.op("vector", lambda e, j=j: e.tensor_copy(
                    rsbb[:, j * 512:(j + 1) * 512], psY[0][:, :]))
                prev = p.mark("vector", dv)
            n_dv_bc = prev
            psY_free[0] = (dv, n_dv_bc)

            p.wait("sync", qg, n_qg_wd)
            p.wait("scalar", qg, n_qg_wd)
            p.wait("sync", qg, n_qg_wo)
            n_wo_k = []
            for k in range(KC):
                for fh2 in range(2):
                    p.dma("sync",
                          woT[:, k * 2048 + fh2 * 1024:
                              k * 2048 + (fh2 + 1) * 1024],
                          wo_bf[fh2 * 1024:(fh2 + 1) * 1024,
                                k * 128:(k + 1) * 128], qd, transpose=True)
                n_wo_k.append(qd.n)
            p.wait("vector", dv, n_dv_bc)
            for k in range(KC):
                p.wait("vector", qd, n_wo_k[k])
                p.op("vector", lambda e, k=k: e.tensor_mul(
                    woT[:, k * 2048:(k + 1) * 2048],
                    woT[:, k * 2048:(k + 1) * 2048], rsbb[:, :]))
            n_dv_woT = p.mark("vector", dv)

            fu_free = 0
            og_free = 0
            wds_free = [(None, 0)] * 4
            xts_free = [0, 0, 0, 0]
            ystg_free = [0, 0]
            ld_done = False

            for tt in range(4):
                p.wait("sync", pe, og_free)
                p.wait("sync", qd, n_qd_p5)
                for k in range(KC):
                    for hf in range(2):
                        b0 = 2 * tt + hf
                        p.dma("sync",
                              ogT[:, k * 512 + hf * 256:k * 512 + hf * 256 + 256],
                              og_scr[b0 * 128:(b0 + 1) * 128,
                                     k * BL:(k + 1) * BL], qd)
                n_ld_og = qd.n
                p.wait("scalar", pe, fu_free)
                p.wait("scalar", qd, n_qd_p6a)
                for k in range(64):
                    p.dma("scalar", fuT[:, k * 512:(k + 1) * 512],
                          fuse_scr[k * 128:(k + 1) * 128,
                                   tt * 512:(tt + 1) * 512], qs)
                n_ld_fu = qs.n

                for fh in range(2):
                    for tsub in range(4):
                        p.wait("sync", dv, xts_free[tsub])
                        p.dma("sync", xts[tsub][:, :],
                              x[tt * 512 + tsub * 128:
                                tt * 512 + (tsub + 1) * 128,
                                fh * 1024:(fh + 1) * 1024], qd)
                    n_ld_x = qd.n
                    p.wait("tensor", qd, n_ld_og)
                    p.wait("tensor", dv, n_dv_woT)
                    first = True
                    for k in range(KC):
                        for tsub in range(4):
                            for foc in range(2):
                                sp = tsub * 2 + foc
                                if first and psY_free[sp][0]:
                                    p.wait("tensor", psY_free[sp][0],
                                           psY_free[sp][1])
                                p.op("tensor",
                                     lambda e, k=k, tsub=tsub, foc=foc, sp=sp,
                                     fr=first, fh=fh:
                                     e.matmul(psY[sp][:, :],
                                              ogT[:, k * 512 + tsub * 128:
                                                  k * 512 + tsub * 128 + 128],
                                              woT[:, k * 2048 + fh * 1024 +
                                                  foc * 512:
                                                  k * 2048 + fh * 1024 +
                                                  (foc + 1) * 512],
                                              start=fr, stop=False))
                        first = False
                    p.wait("tensor", qs, n_ld_fu)
                    for k in range(64):
                        kp = k % 4
                        ldeng, ldsem = (("sync", qd) if k % 2 == 0
                                        else ("scalar", qs))
                        if wds_free[kp][0]:
                            p.wait(ldeng, wds_free[kp][0], wds_free[kp][1])
                        n_wd_ld = p.dma(ldeng, wds[kp][:, :],
                                        wd_bf[k * 128:(k + 1) * 128,
                                              fh * 1024:(fh + 1) * 1024],
                                        ldsem)
                        p.wait("tensor", ldsem, n_wd_ld)
                        for tsub in range(4):
                            for foc in range(2):
                                sp = tsub * 2 + foc
                                p.op("tensor",
                                     lambda e, k=k, kp=kp, tsub=tsub, foc=foc,
                                     sp=sp, last=(k == 63):
                                     e.matmul(psY[sp][:, :],
                                              fuT[:, k * 512 + tsub * 128:
                                                  k * 512 + tsub * 128 + 128],
                                              wds[kp][:, foc * 512:
                                                      (foc + 1) * 512],
                                              start=False, stop=last))
                        n_wd_mm = p.mark("tensor", pe)
                        wds_free[kp] = (pe, n_wd_mm)
                    n_fh_mm = p.mark("tensor", pe)
                    for tsub in range(4):
                        for foc in range(2):
                            sp = tsub * 2 + foc
                            yb = sp % 2
                            p.wait("vector", pe, n_fh_mm)
                            p.wait("vector", qd, n_ld_x)
                            p.wait("vector", qg, ystg_free[yb])
                            p.op("vector",
                                 lambda e, sp=sp, yb=yb, fh=fh, foc=foc:
                                 e.tensor_mul(ystg[yb][:, :], psY[sp][:, :],
                                              sdbb[:, fh * 1024 + foc * 512:
                                                   fh * 1024 +
                                                   (foc + 1) * 512]))
                            n_ev1 = p.mark("vector", dv)
                            psY_free[sp] = (dv, n_ev1)
                            p.op("vector",
                                 lambda e, yb=yb, tsub=tsub, foc=foc:
                                 e.tensor_add(ystg[yb][:, :], ystg[yb][:, :],
                                              xts[tsub][:, foc * 512:
                                                        (foc + 1) * 512]))
                            n_ev2 = p.mark("vector", dv)
                            if foc == 1:
                                xts_free[tsub] = n_ev2
                            p.wait("gpsimd", dv, n_ev2)
                            p.dma("gpsimd",
                                  y[tt * 512 + tsub * 128:
                                    tt * 512 + (tsub + 1) * 128,
                                    fh * 1024 + foc * 512:
                                    fh * 1024 + (foc + 1) * 512],
                                  ystg[yb][:, :], qg)
                            ystg_free[yb] = qg.n
                fu_free = pe.n
                og_free = pe.n
            p.wait("sync", qd, qd.n)
            p.wait("sync", qg, qg.n)
            p.flush()

    if _lvl < 5:
        p.wait("sync", qd, qd.n)
        p.wait("sync", qg, qg.n)
        p.wait("sync", qs, qs.n)
        p.wait("sync", cc, cc.n)
        p.wait("vector", dv, dv.n)
        p.wait("scalar", ac, ac.n)
        p.wait("tensor", pe, pe.n)
        p.wait("gpsimd", pl, pl.n)
        p.flush()
    st.close()
    return nc


_NC_CACHE = {}


def _get_nc():
    if "nc" not in _NC_CACHE:
        _NC_CACHE["nc"] = build_nc()
    return _NC_CACHE["nc"]


def make_in_maps(L0_attn_x, L0_attn_slope_rate, L0_attn_W_qkvu,
                 L0_attn_W_out, L0_ffn_w_gate, L0_ffn_w_up, L0_ffn_w_down,
                 L0_ffn_s_gate, L0_ffn_s_up, L0_ffn_s_down):
    xf = np.ascontiguousarray(np.asarray(L0_attn_x).reshape(NCORES * R, D),
                              dtype=np.float32)
    slope = np.ascontiguousarray(np.asarray(L0_attn_slope_rate).reshape(H),
                                 dtype=np.float32)
    wq = np.ascontiguousarray(np.asarray(L0_attn_W_qkvu), dtype=np.float32)
    wo = np.ascontiguousarray(np.asarray(L0_attn_W_out), dtype=np.float32)
    wg_ = np.ascontiguousarray(np.asarray(L0_ffn_w_gate), dtype=np.int8)
    wu_ = np.ascontiguousarray(np.asarray(L0_ffn_w_up), dtype=np.int8)
    wd_ = np.ascontiguousarray(np.asarray(L0_ffn_w_down), dtype=np.int8)
    sg_ = np.ascontiguousarray(np.asarray(L0_ffn_s_gate).reshape(FI // 128,
                                                                 128),
                               dtype=np.float32)
    su_ = np.ascontiguousarray(np.asarray(L0_ffn_s_up).reshape(FI // 128,
                                                               128),
                               dtype=np.float32)
    sd_ = np.ascontiguousarray(np.asarray(L0_ffn_s_down).reshape(1, D),
                               dtype=np.float32)

    in_maps = []
    for c in range(NCORES):
        pm = np.array([[1.0 if r < c else 0.0 for r in range(NCORES)]],
                      dtype=np.float32)
        pd = np.array([[float(c - 1 - r) if r < c else 0.0
                        for r in range(NCORES)]], dtype=np.float32)
        in_maps.append({
            "x": np.ascontiguousarray(xf[c * R:(c + 1) * R]),
            "slope": slope, "wqkvu": wq, "wout": wo,
            "wg": wg_, "wu": wu_, "wd": wd_,
            "sg": sg_, "su": su_, "sdr": sd_,
            "pmask": pm, "pdist": pd,
        })
    return in_maps


def kernel(**inputs):
    in_maps = make_in_maps(**inputs)
    nc = _get_nc()
    res = run_bass_kernel_spmd(nc, in_maps, core_ids=list(range(NCORES)))
    out = np.concatenate([np.asarray(r["y"]) for r in res.results], axis=0)
    return out.reshape(1, NCORES * R, D).astype(np.float32)


# version 12
# speedup vs baseline: 1.0012x; 1.0012x over previous
import contextlib
import numpy as np

import concourse.bass as bass
import concourse.mybir as mybir
from concourse.bass_utils import run_bass_kernel_spmd

F32 = mybir.dt.float32
BF16 = mybir.dt.bfloat16
I8 = mybir.dt.int8
AF = mybir.ActivationFunctionType
ALU = mybir.AluOpType
AXL = mybir.AxisListType

NCORES = 8
R = 2048
D = 2048
H = 16
HD = 128
B = 8
BL = 256
FI = 8192
KC = D // 128
EPS_LN = 1e-5
EPS_RMS = 1e-6

ENGS = ("sync", "scalar", "vector", "tensor", "gpsimd")


class SemC:
    def __init__(self, h):
        self.h = h
        self.n = 0


class Plan:

    def __init__(self, nc, sems):
        self.nc = nc
        self.sems = sems
        self.ops = {e: [] for e in ENGS}

    def wait(self, eng, sem, val):
        if val is None or val <= 0:
            return
        self.ops[eng].append(("w", sem.h, val))

    def dma(self, eng, out, in_, sem, transpose=False):
        sem.n += 16

        def fn(e, o=out, i=in_, tp=transpose):
            return e.dma_start(out=o, in_=i, transpose=tp)
        self.ops[eng].append(["i", fn, [(sem.h, 16)]])
        return sem.n

    def op(self, eng, fn, sem=None, k=1):
        incs = []
        v = None
        if sem is not None:
            sem.n += k
            v = sem.n
            incs.append((sem.h, k))
        self.ops[eng].append(["i", fn, incs])
        return v

    def mark(self, eng, sem, k=1):
        entry = None
        for e_ in reversed(self.ops[eng]):
            if e_[0] == "i":
                entry = e_
                break
        assert entry is not None, f"mark on empty stream {eng}"
        sem.n += k
        for j, (sh, kk) in enumerate(entry[2]):
            if sh is sem.h:
                entry[2][j] = (sh, kk + k)
                break
        else:
            entry[2].append((sem.h, k))
        return sem.n

    def flush(self):
        with self.nc.Block() as blk:
            for eng in ENGS:
                ops = self.ops[eng]
                if not ops:
                    continue

                def runner(e, _ops=tuple(tuple(o) for o in ops)):
                    for o in _ops:
                        if o[0] == "w":
                            e.wait_ge(o[1], o[2])
                        else:
                            inst = o[1](e)
                            for (s, k) in o[2]:
                                inst.then_inc(s, k)
                getattr(blk, eng)(runner)
        self.ops = {e: [] for e in ENGS}


def build_nc(debug=(), stop_after="full"):
    _lvl = {"p1": 1, "p2": 2, "p35": 3, "p6a": 4, "full": 5}[stop_after]
    nc = bass.Bass("TRN2", target_bir_lowering=False)
    dbg = set(debug)

    def scratch(name, shape, dtype):
        kind = "ExternalOutput" if name in dbg else "Internal"
        return nc.dram_tensor(name, shape, dtype, kind=kind).ap()

    x = nc.dram_tensor("x", [R, D], F32, kind="ExternalInput").ap()
    slope = nc.dram_tensor("slope", [H], F32, kind="ExternalInput").ap()
    wqkvu = nc.dram_tensor("wqkvu", [4 * D, D], F32, kind="ExternalInput").ap()
    wout = nc.dram_tensor("wout", [D, D], F32, kind="ExternalInput").ap()
    wg = nc.dram_tensor("wg", [D, FI], I8, kind="ExternalInput").ap()
    wu = nc.dram_tensor("wu", [D, FI], I8, kind="ExternalInput").ap()
    wd = nc.dram_tensor("wd", [FI, D], I8, kind="ExternalInput").ap()
    sg = nc.dram_tensor("sg", [FI // 128, 128], F32, kind="ExternalInput").ap()
    su = nc.dram_tensor("su", [FI // 128, 128], F32, kind="ExternalInput").ap()
    sdr = nc.dram_tensor("sdr", [1, D], F32, kind="ExternalInput").ap()
    pmask = nc.dram_tensor("pmask", [1, NCORES], F32, kind="ExternalInput").ap()
    pdist = nc.dram_tensor("pdist", [1, NCORES], F32, kind="ExternalInput").ap()
    y = nc.dram_tensor("y", [R, D], F32, kind="ExternalOutput").ap()

    xnt_scr = scratch("xnt_scr", [128, KC * R], BF16)
    wq_bf = scratch("wq_bf", [4 * D, D], BF16)
    wo_bf = scratch("wo_bf", [D, D], BF16)
    wd_bf = scratch("wd_bf", [FI, D], BF16)
    q_scr = scratch("q_scr", [B * 128, H * BL], BF16)
    k_scr = scratch("k_scr", [D, R], BF16)
    k_img = scratch("k_img", [B * 128, H * BL], BF16)
    v_scr = scratch("v_scr", [D, R], BF16)
    u_scr = scratch("u_scr", [B * 128, H * BL], F32)
    og_scr = scratch("og_scr", [B * 128, H * BL], BF16)
    fuse_scr = scratch("fuse_scr", [FI, R], BF16)
    cc_in = scratch("cc_in", [128, H * HD], F32)
    cc_out = scratch("cc_out", [NCORES * 128, H * HD], F32)
    wcb = scratch("wcb", [H, NCORES], F32)
    cc_dbg = scratch("cc_dbg", [128, H * HD], F32)
    sin_dbg = scratch("sin_dbg", [128, H * HD], F32)
    wcomb_dbg = scratch("wcomb_dbg", [128, 128], F32)

    st = contextlib.ExitStack()
    sems = {nm: SemC(st.enter_context(nc.semaphore(nm)))
            for nm in ("qd", "qs", "qg", "pe", "dv", "ac", "pl", "cc")}
    qd, qs, qg = sems["qd"], sems["qs"], sems["qg"]
    pe, dv, ac, pl, cc = (sems["pe"], sems["dv"], sems["ac"], sems["pl"],
                          sems["cc"])

    def sb(name, shape, dtype, stack=st, **kw):
        return stack.enter_context(nc.sbuf_tensor(name, shape, dtype, **kw))

    sgt = sb("sgt", [128, FI // 128], F32)
    sut = sb("sut", [128, FI // 128], F32)

    mid_st = contextlib.ExitStack()
    sbc = sb("sbc", [128, H], F32, stack=mid_st)
    nsbc = sb("nsbc", [128, H], F32, stack=mid_st)
    bdp = sb("bdp", [128, B * H], F32, stack=mid_st)
    iotam = sb("iotam", [128, BL], F32, stack=mid_st)
    idxp = sb("idxp", [128, 2 * BL], F32, stack=mid_st)
    ceps5 = sb("ceps5", [128, 1], F32, stack=mid_st)
    ceps6 = sb("ceps6", [128, 1], F32, stack=mid_st)
    on1 = sb("on1", [1, 128], F32, stack=mid_st)
    onP = sb("onP", [128, 1], BF16, stack=mid_st)
    wcomb = sb("wcomb", [128, 128], F32, stack=mid_st)
    tcol = sb("tcol", [128, 1], F32, stack=mid_st)
    tv = sb("tv", [128, 2], F32, stack=mid_st)
    otile = sb("otile", [128, HD], F32, stack=mid_st)
    Sin = sb("Sin", [128, H * HD], F32, stack=mid_st)
    Sinb = sb("Sinb", [128, H * HD], BF16, stack=mid_st)
    snap = sb("snap", [128, (B - 1) * H * HD], BF16, stack=mid_st)

    p = Plan(nc, sems)

    s_order = [4, 5, 6, 7, 8, 9, 10, 11, 0, 1, 2, 3, 12, 13, 14, 15]

    with contextlib.ExitStack() as pst:
        psq = [pst.enter_context(nc.psum_tensor(f"psq{i}", [128, 512], F32))
               for i in range(4)]
        psT = [pst.enter_context(nc.psum_tensor(f"psT{i}", [128, 1024], BF16))
               for i in range(2)]
        psS = [pst.enter_context(nc.psum_tensor(f"psS{i}", [128, 512], F32))
               for i in range(2)]
        id128 = sb("id128", [128, 128], F32, stack=pst)
        id128b = sb("id128b", [128, 128], BF16, stack=pst)
        bdvec = sb("bdvec", [128, H], F32, stack=pst)
        stv = sb("stv", [128, 2 * H], F32, stack=pst)
        xnt = sb("xnt", [128, KC * R], BF16, stack=pst)
        wts = [sb(f"wts{i}", [128, KC * 512], BF16, stack=pst,
                  is_transpose_dest=True) for i in range(2)]
        stg = [sb(f"stg{i}", [128, R], BF16, stack=pst) for i in range(2)]
        ustg = sb("ustg", [128, R], F32, stack=pst)

        p.op("gpsimd", lambda e: e.memset(ceps5[:, :], EPS_LN))
        p.op("gpsimd", lambda e: e.memset(ceps6[:, :], EPS_RMS))
        p.op("gpsimd", lambda e: e.memset(on1[:, :], 1.0))
        p.op("gpsimd", lambda e: e.memset(onP[:, :], 1.0))
        p.op("gpsimd", lambda e: e.memset(otile[:, :], 1.0))
        p.op("gpsimd", lambda e: e.memset(id128[:, :], 1.0))
        p.op("gpsimd", lambda e: e.iota(iotam[:, :], [[1, BL]], base=0,
                                        channel_multiplier=0,
                                        allow_small_or_imprecise_dtypes=True))
        p.op("gpsimd", lambda e: e.iota(tcol[:, :], [[1, 1]], base=0,
                                        channel_multiplier=1,
                                        allow_small_or_imprecise_dtypes=True))
        p.op("gpsimd", lambda e: e.affine_select(
            id128[:, :], id128[:, :], [[1, 128]], ALU.is_equal, 0.0,
            base=0, channel_multiplier=-1))
        for c in range(2):
            p.op("gpsimd", lambda e, c=c: e.iota(
                idxp[:, c * BL:(c + 1) * BL], [[1, BL]], base=-(c * 128),
                channel_multiplier=-1, allow_small_or_imprecise_dtypes=True))
            p.op("gpsimd", lambda e, c=c: e.affine_select(
                idxp[:, c * BL:(c + 1) * BL], idxp[:, c * BL:(c + 1) * BL],
                [[1, BL]], ALU.is_ge, 1.0e30,
                base=-(c * 128), channel_multiplier=-1))
        n_pl0 = p.mark("gpsimd", pl)

        n_qg_wq = {}
        for s_ in s_order[:2]:
            n_qg_wq[s_] = p.dma("gpsimd",
                                wq_bf[s_ * 512:(s_ + 1) * 512, :],
                                wqkvu[s_ * 512:(s_ + 1) * 512, :], qg)

        with contextlib.ExitStack() as p01:
            srow = sb("srow", [1, H], F32, stack=p01)
            s16 = sb("s16", [H, 1], F32, stack=p01)
            pmr = sb("pmr", [1, NCORES], F32, stack=p01)
            pdr = sb("pdr", [1, NCORES], F32, stack=p01)
            sg64 = sb("sg64", [64, 128], F32, stack=p01)
            su64 = sb("su64", [64, 128], F32, stack=p01)
            w16 = sb("w16", [H, NCORES], F32, stack=p01)
            wrow = sb("wrow", [1, H * NCORES], F32, stack=p01)
            xt = [sb(f"xt{i}", [128, D], F32, stack=p01) for i in range(3)]
            xnb = [sb(f"xnb{i}", [128, D], BF16, stack=p01) for i in range(3)]
            stat = [sb(f"stat{i}", [128, 8], F32, stack=p01) for i in range(3)]

            p.dma("sync", srow[:, :], slope.rearrange("(o h) -> o h", o=1), qd)
            p.dma("sync", s16[:, :], slope.rearrange("(h o) -> h o", o=1), qd)
            p.dma("sync", pmr[:, :], pmask, qd)
            p.dma("sync", pdr[:, :], pdist, qd)
            p.dma("sync", sg64[:, :], sg, qd)
            n_qd_consts = p.dma("sync", su64[:, :], su, qd)

            n_wslab01 = []
            for pp in range(2):
                s0 = s_order[pp]
                p.wait("sync", qg, n_qg_wq[s0])
                for k in range(KC):
                    p.dma("sync", wts[pp][:, k * 512:(k + 1) * 512],
                          wq_bf[s0 * 512:(s0 + 1) * 512,
                                k * 128:(k + 1) * 128], qd, transpose=True)
                n_wslab01.append(qd.n)

            psA, psB = psq[0], psq[1]
            p.wait("tensor", qd, n_qd_consts)
            p.wait("tensor", pl, n_pl0)
            p.op("tensor", lambda e: e.matmul(psA[:, 0:16], on1[:, :],
                                              srow[:, :], start=True, stop=True))
            p.op("tensor", lambda e: e.matmul(psA[:, 16:24], on1[:, :],
                                              pmr[:, :], start=True, stop=True))
            p.op("tensor", lambda e: e.matmul(psA[:, 24:32], on1[:, :],
                                              pdr[:, :], start=True, stop=True))
            p.op("tensor", lambda e: e.transpose(psA[:, 32:96], sg64[:, :],
                                                 id128[0:64, 0:64]))
            p.op("tensor", lambda e: e.transpose(psA[:, 96:160], su64[:, :],
                                                 id128[0:64, 0:64]))
            n_pe0 = p.mark("tensor", pe)

            p.wait("vector", pe, n_pe0)
            p.op("vector", lambda e: e.tensor_copy(sbc[:, :], psA[:, 0:16]))
            p.op("vector", lambda e: e.tensor_scalar_mul(nsbc[:, :],
                                                         psA[:, 0:16], -1.0))
            p.op("vector", lambda e: e.tensor_copy(sgt[:, :], psA[:, 32:96]))
            p.op("vector", lambda e: e.tensor_copy(sut[:, :], psA[:, 96:160]))
            p.op("vector", lambda e: e.tensor_scalar_mul(w16[:, :],
                                                         psA[0:16, 24:32],
                                                         s16[:, :]))
            p.op("vector", lambda e: e.tensor_copy(id128b[:, :], id128[:, :]))
            p.wait("vector", pl, n_pl0)
            for c in range(2):
                p.op("vector", lambda e, c=c: e.tensor_scalar_add(
                    tv[:, c:c + 1], tcol[:, :], float(128 * c - 255)))
            for c in range(2):
                for h in range(H):
                    p.op("vector", lambda e, c=c, h=h: e.tensor_mul(
                        stv[:, c * H + h:c * H + h + 1], tv[:, c:c + 1],
                        sbc[:, h:h + 1]))
            p.op("vector", lambda e: e.memset(Sin[:, :], 0.0))
            n_dv_ev0 = p.mark("vector", dv)

            p.wait("scalar", dv, n_dv_ev0)
            p.op("scalar", lambda e: e.activation(bdvec[:, :], sbc[:, :],
                                                  AF.Exp, scale=-256.0))
            for b_ in range(B):
                p.op("scalar", lambda e, b_=b_: e.activation(
                    bdp[:, b_ * H:(b_ + 1) * H], sbc[:, :], AF.Exp,
                    scale=-256.0 * b_))
            p.op("scalar", lambda e: e.activation(w16[:, :], w16[:, :],
                                                  AF.Exp, scale=-2048.0))
            n_ac_w16 = p.mark("scalar", ac)
            p.wait("vector", ac, n_ac_w16)
            p.op("vector", lambda e: e.tensor_mul(w16[:, :], w16[:, :],
                                                  psA[0:16, 16:24]))
            n_dv_w16 = p.mark("vector", dv)

            p.wait("sync", dv, n_dv_w16)
            nb1 = p.dma("sync", wcb, w16[:, :], qd)
            p.wait("sync", qd, nb1)
            nb2 = p.dma("sync", wrow[:, :],
                        wcb.rearrange("h r -> (h r)").rearrange(
                            "(o n) -> o n", o=1), qd)
            p.wait("tensor", qd, nb2)
            p.op("tensor", lambda e: e.matmul(psB[:, 0:128], on1[:, :],
                                              wrow[:, :], start=True, stop=True))
            n_pe_wc = p.mark("tensor", pe)
            p.wait("vector", pe, n_pe_wc)
            p.op("vector", lambda e: e.tensor_copy(wcomb[:, :], psB[:, 0:128]))
            n_dv_wc = p.mark("vector", dv)
            psq_init_free = [(dv, n_dv_w16), (dv, n_dv_wc), (None, 0), (None, 0)]

            xt_free = [(None, 0)] * 3
            xt_free2 = [(None, 0)] * 3
            xn_free_pe = [0, 0, 0]
            stat_free = [(None, 0)] * 3
            psT_free = [(None, 0), (None, 0)]
            for i in range(16):
                par = i % 3
                eng, qsem = "scalar", qs
                if xt_free[par][0]:
                    p.wait(eng, xt_free[par][0], xt_free[par][1])
                if xt_free2[par][0]:
                    p.wait(eng, xt_free2[par][0], xt_free2[par][1])
                n_ld = p.dma(eng, xt[par][:, :], x[i * 128:(i + 1) * 128, :],
                             qsem)
                p.wait("vector", qsem, n_ld)
                p.op("vector", lambda e, par=par: e.reduce_sum(
                    stat[par][:, 0:1], xt[par][:, :], axis=AXL.X))
                n_sum = p.mark("vector", dv)
                p.wait("scalar", qsem, n_ld)
                if stat_free[par][0]:
                    p.wait("scalar", stat_free[par][0], stat_free[par][1])
                p.op("scalar", lambda e, par=par: e.activation(
                    ustg[:, :], xt[par][:, :], AF.Square,
                    accum_out=stat[par][:, 2:3]))
                p.wait("scalar", dv, n_sum)
                p.op("scalar", lambda e, par=par: e.mul(
                    stat[par][:, 1:2], stat[par][:, 0:1], 1.0 / D))
                p.op("scalar", lambda e, par=par: e.activation(
                    stat[par][:, 3:4], stat[par][:, 0:1], AF.Square))
                n_ss = p.mark("scalar", ac)
                p.wait("vector", ac, n_ss)
                p.op("vector", lambda e, par=par: e.scalar_tensor_tensor(
                    stat[par][:, 4:5], stat[par][:, 3:4], -1.0 / D,
                    stat[par][:, 2:3], ALU.mult, ALU.add))
                n_var = p.mark("vector", dv)
                p.wait("scalar", dv, n_var)
                p.op("scalar", lambda e, par=par: e.activation(
                    stat[par][:, 5:6], stat[par][:, 4:5], AF.Ln, scale=1.0 / D,
                    bias=ceps5[:, :]))
                p.op("scalar", lambda e, par=par: e.activation(
                    stat[par][:, 6:7], stat[par][:, 5:6], AF.Exp, scale=-0.5))
                n_rstd = p.mark("scalar", ac)
                p.wait("vector", ac, n_rstd)
                p.wait("vector", pe, xn_free_pe[par])
                p.op("vector", lambda e, par=par: e.tensor_scalar(
                    xnb[par][:, :], xt[par][:, :], stat[par][:, 1:2],
                    stat[par][:, 6:7], ALU.subtract, ALU.mult))
                n_xn = p.mark("vector", dv)
                xt_free[par] = (dv, n_xn)
                xt_free2[par] = (ac, n_ss)
                stat_free[par] = (dv, n_xn)
                p.wait("tensor", dv, n_xn)
                for half in range(2):
                    tb = half
                    if psT_free[tb][0]:
                        p.wait("tensor", psT_free[tb][0], psT_free[tb][1])
                    for k in range(8):
                        kk = half * 8 + k
                        p.op("tensor", lambda e, par=par, tb=tb, k=k, kk=kk:
                             e.transpose(psT[tb][:, k * 128:(k + 1) * 128],
                                         xnb[par][:, kk * 128:(kk + 1) * 128],
                                         id128b[:, :]))
                    n_tp = p.mark("tensor", pe)
                    p.wait("vector", pe, n_tp)
                    for k in range(8):
                        kk = half * 8 + k
                        p.op("vector", lambda e, tb=tb, k=k, kk=kk, i=i:
                             e.tensor_copy(
                                 xnt[:, kk * R + i * 128:kk * R + i * 128 + 128],
                                 psT[tb][:, k * 128:(k + 1) * 128]))
                    n_ev = p.mark("vector", dv)
                    psT_free[tb] = (dv, n_ev)
                xn_free_pe[par] = pe.n
            n_dv_xnt = dv.n

        ktok = [sb(f"ktok{i}", [128, 2 * D], BF16, stack=pst,
                   is_transpose_dest=True) for i in range(1)]
        vtok3 = [sb(f"vtok3{i}", [128, 2 * D], BF16, stack=pst,
                    is_transpose_dest=True) for i in range(1)]
        kd_bc = [sb(f"kdbc{c}", [128, D], BF16, stack=pst) for c in range(2)]
        S = sb("S", [128, H * HD], F32, stack=pst)

        for c in range(2):
            for h in range(H):
                p.op("scalar", lambda e, c=c, h=h: e.activation(
                    kd_bc[c][:, h * HD:(h + 1) * HD], otile[:, :], AF.Exp,
                    scale=stv[:, c * H + h:c * H + h + 1]))
        n_ac_kd = p.mark("scalar", ac)
        p.wait("vector", ac, n_ac_kd)
        p.op("vector", lambda e: e.memset(S[:, :], 0.0))
        n_dv_s0 = p.mark("vector", dv)

        wts_free = [0, 0]
        stg_free = [None, None]
        ustg_free = None
        psum_free = list(psq_init_free)
        slot_i = 0
        n_qs_kdone = None
        n_qs_vdone = None
        p3_tok_free = 0
        p3_ps_free = [(None, 0), (None, 0)]
        p3_upd_prev = n_dv_s0
        p3_n_tok = 0
        n_qd_cc = None

        def emit_p3_block(b_):
            nonlocal_vals = None

        p3_state = {"tok_free": [0], "upd_prev": n_dv_s0,
                    "ps_free": [(None, 0), (None, 0)]}

        def p3_block(b_):
            st3 = p3_state
            bp = 0
            kt, vt = ktok[bp], vtok3[bp]
            p.wait("sync", pe, st3["tok_free"][bp])
            p.wait("sync", qs, n_qs_vdone)
            for c in range(2):
                p.dma("sync", kt[:, c * D:(c + 1) * D],
                      k_scr[:, b_ * BL + c * 128:b_ * BL + (c + 1) * 128],
                      qd, transpose=True)
                p.dma("sync", vt[:, c * D:(c + 1) * D],
                      v_scr[:, b_ * BL + c * 128:b_ * BL + (c + 1) * 128],
                      qd, transpose=True)
            n_tok = qd.n
            p.wait("vector", qd, n_tok)
            p.wait("vector", pe, st3["tok_free"][bp])
            for c in range(2):
                p.op("vector", lambda e, c=c, vt=vt: e.tensor_mul(
                    vt[:, c * D:(c + 1) * D], vt[:, c * D:(c + 1) * D],
                    kd_bc[c][:, :]))
            if b_ >= 1:
                p.op("vector", lambda e, b_=b_: e.tensor_copy(
                    snap[:, (b_ - 1) * H * HD:b_ * H * HD], S[:, :]))
            n_ksc = p.mark("vector", dv)
            for h in range(H):
                hp = h % 2
                p.wait("tensor", dv, n_ksc)
                if st3["ps_free"][hp][0]:
                    p.wait("tensor", st3["ps_free"][hp][0],
                           st3["ps_free"][hp][1])
                for c in range(2):
                    p.op("tensor", lambda e, hp=hp, c=c, h=h, kt=kt, vt=vt:
                         e.matmul(psS[hp][:, 0:HD],
                                  kt[:, c * D + h * HD:c * D + (h + 1) * HD],
                                  vt[:, c * D + h * HD:c * D + (h + 1) * HD],
                                  start=(c == 0), stop=(c == 1)))
                n_mm = p.mark("tensor", pe)
                p.wait("vector", pe, n_mm)
                p.op("vector", lambda e, hp=hp, h=h: e.scalar_tensor_tensor(
                    S[:, h * HD:(h + 1) * HD], S[:, h * HD:(h + 1) * HD],
                    bdvec[:, h:h + 1], psS[hp][:, 0:HD], ALU.mult, ALU.add))
                n_upd = p.mark("vector", dv)
                st3["ps_free"][hp] = (dv, n_upd)
                st3["upd_prev"] = n_upd
            st3["tok_free"][bp] = pe.n

        n_spill_xnt = 0
        wts_loaded = {0: n_wslab01[0], 1: n_wslab01[1]}

        def emit_wts_loads(pp):
            s2_ = s_order[pp]
            par_ = pp % 2
            p.wait("sync", qg, n_qg_wq[s2_])
            p.wait("sync", pe, wts_free[par_])
            for k in range(KC):
                p.dma("sync", wts[par_][:, k * 512:(k + 1) * 512],
                      wq_bf[s2_ * 512:(s2_ + 1) * 512,
                            k * 128:(k + 1) * 128],
                      qd, transpose=True)
            wts_loaded[pp] = qd.n

        for pos, s_ in enumerate(s_order):
            par = pos % 2
            kind = ("q", "k", "v", "u")[s_ // 4]
            chunk0 = (s_ % 4) * 4
            n_wslab = wts_loaded[pos]
            p.wait("tensor", qd, n_wslab)
            p.wait("tensor", dv, n_dv_xnt)
            for fc in range(4):
                chunk = chunk0 + fc
                ev_marks = []
                for t in range(4):
                    slot = slot_i % 4
                    slot_i += 1
                    if psum_free[slot][0]:
                        p.wait("tensor", psum_free[slot][0],
                               psum_free[slot][1])
                    for k in range(KC):
                        p.op("tensor",
                             lambda e, slot=slot, par=par, fc=fc, t=t, k=k:
                             e.matmul(psq[slot][:, :],
                                      wts[par][:, k * 512 + fc * 128:
                                               k * 512 + fc * 128 + 128],
                                      xnt[:, k * R + t * 512:
                                          k * R + (t + 1) * 512],
                                      start=(k == 0), stop=(k == KC - 1)))
                    n_mm = p.mark("tensor", pe)
                    if kind in ("q", "k"):
                        p.wait("scalar", pe, n_mm)
                        if t == 0 and stg_free[par]:
                            p.wait("scalar", qs, stg_free[par][1])
                        p.op("scalar",
                             lambda e, slot=slot, par=par, t=t: e.activation(
                                 stg[par][:, t * 512:(t + 1) * 512],
                                 psq[slot][:, :], AF.Silu))
                        nev = p.mark("scalar", ac)
                        psum_free[slot] = (ac, nev)
                    else:
                        p.wait("vector", pe, n_mm)
                        if t == 0:
                            fr = stg_free[par] if kind == "v" else ustg_free
                            if fr:
                                p.wait("vector", qs, fr[1])
                        if kind == "v":
                            p.op("vector",
                                 lambda e, slot=slot, par=par, t=t:
                                 e.tensor_copy(
                                     stg[par][:, t * 512:(t + 1) * 512],
                                     psq[slot][:, :]))
                        else:
                            p.op("vector",
                                 lambda e, slot=slot, t=t:
                                 e.tensor_copy(
                                     ustg[:, t * 512:(t + 1) * 512],
                                     psq[slot][:, :]))
                        nev = p.mark("vector", dv)
                        psum_free[slot] = (dv, nev)
                    ev_marks.append(psum_free[slot])
                last = ev_marks[-1]
                p.wait("scalar", last[0], last[1])
                if kind == "q":
                    for b_ in range(B):
                        p.dma("scalar",
                              q_scr[b_ * 128:(b_ + 1) * 128,
                                    chunk * BL:(chunk + 1) * BL],
                              stg[par][:, b_ * BL:(b_ + 1) * BL], qs)
                    stg_free[par] = ("qs", qs.n)
                elif kind == "k":
                    p.dma("scalar", k_scr[chunk * 128:(chunk + 1) * 128, :],
                          stg[par][:, :], qs)
                    for b_ in range(B):
                        p.dma("scalar",
                              k_img[b_ * 128:(b_ + 1) * 128,
                                    chunk * BL:(chunk + 1) * BL],
                              stg[par][:, b_ * BL:(b_ + 1) * BL], qs)
                    stg_free[par] = ("qs", qs.n)
                elif kind == "v":
                    p.dma("scalar", v_scr[chunk * 128:(chunk + 1) * 128, :],
                          stg[par][:, :], qs)
                    stg_free[par] = ("qs", qs.n)
                else:
                    for b_ in range(B):
                        p.dma("scalar",
                              u_scr[b_ * 128:(b_ + 1) * 128,
                                    chunk * BL:(chunk + 1) * BL],
                              ustg[:, b_ * BL:(b_ + 1) * BL], qs)
                    ustg_free = ("qs", qs.n)
            wts_free[par] = pe.n
            if 2 <= pos + 1 < 16:
                emit_wts_loads(pos + 1)
            if pos + 2 < 16:
                s2 = s_order[pos + 2]
                p.wait("gpsimd", qd, n_wslab)
                n_qg_wq[s2] = p.dma("gpsimd",
                                    wq_bf[s2 * 512:(s2 + 1) * 512, :],
                                    wqkvu[s2 * 512:(s2 + 1) * 512, :], qg)
            if pos == 3:
                n_qs_kdone = qs.n
            if pos == 7:
                n_qs_vdone = qs.n
            if pos == 8:
                p.wait("sync", dv, n_dv_xnt)
                for j in range(4):
                    p.dma("sync", xnt_scr[:, j * 8192:(j + 1) * 8192],
                          xnt[:, j * 8192:(j + 1) * 8192], qd)
                n_spill_xnt = qd.n
            if 8 <= pos <= 11:
                p3_block(2 * (pos - 8))
                p3_block(2 * (pos - 8) + 1)
            if pos == 11:
                p.wait("sync", dv, p3_state["upd_prev"])
                p.dma("sync", cc_in, S[:, :], qd)
                n_qd_cc = qd.n
                if "cc_dbg" in dbg:
                    p.dma("sync", cc_dbg, S[:, :], qd)
                    p.wait("sync", qd, qd.n)
                p.wait("gpsimd", qd, n_qd_cc)
                p.op("gpsimd", lambda e: e.collective_compute(
                    "AllGather", ALU.bypass,
                    replica_groups=[list(range(NCORES))],
                    ins=[cc_in], outs=[cc_out]), cc)
        n_cc = cc.n

        gr4 = [sb(f"gr4{i}", [128, H * HD], F32, stack=pst) for i in range(1)]
        Sin2 = sb("Sin2", [128, H * HD], F32, stack=pst)
        p.op("vector", lambda e: e.memset(Sin2[:, :], 0.0))
        p.mark("vector", dv)
        p.wait("gpsimd", cc, n_cc)
        gr_free = [0]
        for r in range(NCORES):
            par4 = 0
            p.wait("gpsimd", dv, gr_free[par4])
            n_ld = p.dma("gpsimd", gr4[par4][:, :],
                         cc_out[r * 128:(r + 1) * 128, :], qg)
            src_ = Sin if r % 2 == 0 else Sin2
            dst = Sin2 if r % 2 == 0 else Sin
            p.wait("vector", qg, n_ld)
            for h in range(H):
                p.op("vector",
                     lambda e, par4=par4, r=r, h=h, s=src_, d=dst:
                     e.scalar_tensor_tensor(
                         d[:, h * HD:(h + 1) * HD],
                         gr4[par4][:, h * HD:(h + 1) * HD],
                         wcomb[:, h * NCORES + r:h * NCORES + r + 1],
                         s[:, h * HD:(h + 1) * HD], ALU.mult, ALU.add))
            gr_free[par4] = p.mark("vector", dv)
        p.op("vector", lambda e: e.tensor_copy(Sinb[:, :], Sin[:, :]))
        n_dv_sin = p.mark("vector", dv)
        if "sin_dbg" in dbg:
            p.wait("sync", dv, n_dv_sin)
            p.dma("sync", sin_dbg, Sin[:, :], qd)
            p.dma("sync", wcomb_dbg, wcomb[:, :], qd)

        n_qd_p2 = qd.n
        n_qs_p2 = qs.n
        p.flush()

    if _lvl >= 3:
        with contextlib.ExitStack() as pst:
            qdt = sb("qdt", [128, H * BL], BF16, stack=pst)
            mskt = sb("mskt", [128, 2 * H * BL], BF16, stack=pst)

            p.wait("scalar", dv, n_dv_sin)
            p.wait("sync", dv, n_dv_sin)

            p.wait("scalar", pl, n_pl0)
            for h in range(H):
                p.op("scalar", lambda e, h=h: e.activation(
                    qdt[:, h * BL:(h + 1) * BL], iotam[:, :], AF.Exp,
                    scale=nsbc[:, h:h + 1], bias=nsbc[:, h:h + 1]))
            for h in range(H):
                for c in range(2):
                    p.op("scalar", lambda e, c=c, h=h: e.activation(
                        mskt[:, (h * 2 + c) * BL:(h * 2 + c + 1) * BL],
                        idxp[:, c * BL:(c + 1) * BL], AF.Exp,
                        scale=nsbc[:, h:h + 1]))
            n_ac_tab = p.mark("scalar", ac)
            p.wait("sync", qs, n_qs_p2)
            p.wait("scalar", qs, n_qs_p2)

            with contextlib.ExitStack() as p5st:
                psO = [p5st.enter_context(
                    nc.psum_tensor(f"psO{i}", [128, 512], F32))
                    for i in range(3)]
                psQK = [p5st.enter_context(
                    nc.psum_tensor(f"psQK{i}", [128, 512], F32))
                    for i in range(3)]
                psSS = p5st.enter_context(
                    nc.psum_tensor("psSS", [128, 512], F32))
                psR = p5st.enter_context(
                    nc.psum_tensor("psR", [128, 512], F32))
                qT = [sb(f"qT{i}", [128, H * BL], BF16, stack=p5st)
                      for i in range(2)]
                kT = [sb(f"kT{i}", [128, H * BL], BF16, stack=p5st)
                      for i in range(2)]
                vtok = [sb(f"vtok{i}", [128, 2 * D], BF16, stack=p5st,
                           is_transpose_dest=True) for i in range(2)]
                ublk = sb("ublk", [128, H * BL], F32, stack=p5st)
                ob = sb("ob", [128, H * BL], F32, stack=p5st)
                sqog = sb("sqog", [128, H * BL], BF16, stack=p5st)
                qtl = [sb(f"qtl{i}", [128, H * BL], BF16, stack=p5st)
                       for i in range(2)]
                gall = [sb(f"gall{i}", [128, H * HD], BF16, stack=p5st)
                        for i in range(2)]
                bdpS = sb("bdpS", [128, H * HD], BF16, stack=p5st)
                qkm = [sb(f"qkm{i}", [128, 2 * BL], BF16, stack=p5st)
                       for i in range(4)]
                rstd5 = sb("rstd5", [1, BL], F32, stack=p5st)

                p.wait("vector", ac, n_ac_tab)

                blk_free_pe = [0, 0]
                blk_free_dv = [0, 0]
                u_free_dv = 0
                ob_free_dv = 0
                sqog_free_qd = 0
                psO_free = [(None, 0)] * 3
                psQK_free = [(None, 0)] * 3
                qtl_free = [0, 0]
                gall_free = [0, 0]
                bdpS_free = 0
                qkm_free = [0, 0, 0, 0]
                psSS_free = (None, 0)
                psR_free = (None, 0)
                last_ssmm = 0

                for b_ in range(B):
                    par = b_ % 2
                    p.wait("sync", pe, blk_free_pe[par])
                    p.wait("sync", dv, blk_free_dv[par])
                    p.dma("sync", kT[par][:, :],
                          k_img[b_ * 128:(b_ + 1) * 128, :], qd)
                    for c in range(2):
                        p.dma("sync", vtok[par][:, c * D:(c + 1) * D],
                              v_scr[:, b_ * BL + c * 128:
                                    b_ * BL + (c + 1) * 128],
                              qd, transpose=True)
                    n_tok = qd.n
                    p.wait("scalar", pe, blk_free_pe[par])
                    p.wait("scalar", dv, blk_free_dv[par])
                    n_q = p.dma("scalar", qT[par][:, :],
                                q_scr[b_ * 128:(b_ + 1) * 128, :], qs)
                    p.wait("scalar", dv, u_free_dv)
                    n_u = p.dma("scalar", ublk[:, :],
                                u_scr[b_ * 128:(b_ + 1) * 128, :], qs)

                    p.wait("vector", qs, n_q)
                    p.wait("vector", pe, qtl_free[par])
                    p.op("vector", lambda e, par=par: e.tensor_mul(
                        qtl[par][:, :], qT[par][:, :], qdt[:, :]))
                    n_qtl = p.mark("vector", dv)
                    if b_ >= 1:
                        p.wait("scalar", dv, bdpS_free)
                        for h in range(H):
                            p.op("scalar", lambda e, h=h, b_=b_:
                                 e.activation(
                                     bdpS[:, h * HD:(h + 1) * HD],
                                     Sin[:, h * HD:(h + 1) * HD], AF.Copy,
                                     scale=bdp[:, b_ * H + h:b_ * H + h + 1]))
                        n_bs = p.mark("scalar", ac)
                        p.wait("vector", ac, n_bs)
                        p.wait("vector", pe, gall_free[par])
                        p.op("vector", lambda e, par=par, b_=b_: e.tensor_add(
                            gall[par][:, :], bdpS[:, :],
                            snap[:, (b_ - 1) * H * HD:b_ * H * HD]))
                        n_gall = p.mark("vector", dv)
                        bdpS_free = n_gall
                    else:
                        n_gall = 0

                    ev_marks = []
                    for pr in range(H // 2):
                        hh = (2 * pr, 2 * pr + 1)
                        sl = ((2 * pr) % 3, (2 * pr + 1) % 3)
                        p.wait("tensor", dv, max(n_gall, n_qtl))
                        p.wait("tensor", qd, n_tok)
                        p.wait("tensor", qs, n_q)
                        for j in range(2):
                            h, hp = hh[j], sl[j]
                            if psO_free[hp][0]:
                                p.wait("tensor", psO_free[hp][0],
                                       psO_free[hp][1])
                            glhs = (gall[par] if b_ >= 1 else Sinb)
                            p.op("tensor", lambda e, hp=hp, par=par, h=h,
                                 g=glhs:
                                 e.matmul(psO[hp][:, 0:BL],
                                          g[:, h * HD:(h + 1) * HD],
                                          qtl[par][:, h * BL:(h + 1) * BL],
                                          start=True, stop=False))
                            if psQK_free[hp][0]:
                                p.wait("tensor", psQK_free[hp][0],
                                       psQK_free[hp][1])
                            for c in range(2):
                                p.op("tensor",
                                     lambda e, par=par, hp=hp, c=c, h=h:
                                     e.matmul(
                                         psQK[hp][:, c * BL:(c + 1) * BL],
                                         kT[par][:, h * BL + c * 128:
                                                 h * BL + (c + 1) * 128],
                                         qT[par][:, h * BL:(h + 1) * BL],
                                         start=True, stop=True))
                        n_qk = p.mark("tensor", pe)
                        qtl_free[par] = n_qk
                        gall_free[par] = n_qk
                        p.wait("vector", pe, n_qk)
                        for j in range(2):
                            h, hp = hh[j], sl[j]
                            qi = (pr % 2) * 2 + j
                            p.wait("vector", pe, qkm_free[qi])
                            p.op("vector", lambda e, hp=hp, qi=qi, h=h:
                                 e.tensor_mul(
                                     qkm[qi][:, :], psQK[hp][:, :],
                                     mskt[:, h * 2 * BL:(h + 1) * 2 * BL]))
                        n_qkm = p.mark("vector", dv)
                        for j in range(2):
                            psQK_free[sl[j]] = (dv, n_qkm)
                        p.wait("tensor", dv, n_qkm)
                        for j in range(2):
                            h, hp = hh[j], sl[j]
                            qi = (pr % 2) * 2 + j
                            for c in range(2):
                                p.op("tensor",
                                     lambda e, par=par, hp=hp, qi=qi, c=c, h=h:
                                     e.matmul(psO[hp][:, 0:BL],
                                              vtok[par][:, c * D + h * HD:
                                                        c * D + (h + 1) * HD],
                                              qkm[qi][:, c * BL:(c + 1) * BL],
                                              start=False, stop=(c == 1)))
                        n_diag = p.mark("tensor", pe)
                        qkm_free[(pr % 2) * 2] = n_diag
                        qkm_free[(pr % 2) * 2 + 1] = n_diag
                        p.wait("scalar", pe, n_diag)
                        if pr == 0:
                            p.wait("scalar", dv, ob_free_dv)
                        for j in range(2):
                            h, hp = hh[j], sl[j]
                            p.op("scalar", lambda e, hp=hp, h=h: e.copy(
                                ob[:, h * BL:(h + 1) * BL], psO[hp][:, 0:BL]))
                        nev = p.mark("scalar", ac)
                        for j in range(2):
                            psO_free[sl[j]] = (ac, nev)
                        ev_marks.append(nev)
                    blk_free_pe[par] = pe.n
                    blk_free_dv[par] = dv.n

                    n_all_ev = ev_marks[-1]
                    p.wait("scalar", pe, last_ssmm)
                    p.wait("scalar", qd, sqog_free_qd)
                    p.op("scalar", lambda e: e.activation(
                        sqog[:, :], ob[:, :], AF.Square))
                    n_sq = p.mark("scalar", ac)
                    p.wait("tensor", ac, n_sq)
                    if psSS_free[0]:
                        p.wait("tensor", psSS_free[0], psSS_free[1])
                    for h in range(H):
                        p.op("tensor", lambda e, h=h: e.matmul(
                            psSS[0:1, 0:BL], onP[:, :],
                            sqog[:, h * BL:(h + 1) * BL],
                            start=(h == 0), stop=(h == H - 1)))
                    n_ssmm = p.mark("tensor", pe)
                    last_ssmm = n_ssmm
                    p.wait("scalar", pe, n_ssmm)
                    p.op("scalar", lambda e: e.activation(
                        rstd5[:, :], psSS[0:1, 0:BL], AF.Ln, scale=1.0 / D,
                        bias=ceps6[0:1, :]))
                    p.op("scalar", lambda e: e.activation(
                        rstd5[:, :], rstd5[:, :], AF.Exp, scale=-0.5))
                    n_rstd = p.mark("scalar", ac)
                    psSS_free = (ac, n_rstd)
                    p.wait("tensor", ac, n_rstd)
                    if psR_free[0]:
                        p.wait("tensor", psR_free[0], psR_free[1])
                    p.op("tensor", lambda e: e.matmul(
                        psR[:, 0:BL], on1[:, :], rstd5[:, :], start=True,
                        stop=True))
                    n_rbc = p.mark("tensor", pe)
                    p.wait("vector", pe, n_rbc)
                    p.wait("vector", qs, n_u)
                    p.wait("vector", ac, n_all_ev)
                    p.op("vector", lambda e: e.tensor_mul(
                        ob[:, :], ob[:, :], ublk[:, :]))
                    for h in range(H):
                        p.op("vector", lambda e, h=h: e.tensor_mul(
                            sqog[:, h * BL:(h + 1) * BL],
                            ob[:, h * BL:(h + 1) * BL], psR[:, 0:BL]))
                    n_gat = p.mark("vector", dv)
                    psR_free = (dv, n_gat)
                    ob_free_dv = n_gat
                    u_free_dv = n_gat
                    p.wait("sync", dv, n_gat)
                    p.dma("sync", og_scr[b_ * 128:(b_ + 1) * 128, :],
                          sqog[:, :], qd)
                    sqog_free_qd = qd.n
                    p.wait("gpsimd", qd, sqog_free_qd)
                    p.dma("gpsimd", wd_bf[b_ * 1024:(b_ + 1) * 1024, :],
                          wd[b_ * 1024:(b_ + 1) * 1024, :], qg)
                    if b_ >= 4:
                        s_ = b_ - 4
                        p.dma("gpsimd", wo_bf[s_ * 512:(s_ + 1) * 512, :],
                              wout[s_ * 512:(s_ + 1) * 512, :], qg)
                n_qg_wd = qg.n
                n_qg_wo = qg.n
                n_qd_p5 = qd.n
                n_dv_p5 = dv.n
                n_ac_p5 = ac.n
                n_pe_p5 = pe.n

                p.flush()

    mid_st.close()

    if _lvl >= 4:
        with contextlib.ExitStack() as pst:
            psG = [pst.enter_context(nc.psum_tensor(f"psG{i}", [128, 512], F32))
                   for i in range(2)]
            psU = [pst.enter_context(nc.psum_tensor(f"psU{i}", [128, 512], F32))
                   for i in range(2)]
            xnt2 = sb("xnt2", [128, KC * R], BF16, stack=pst)
            wgs = [sb(f"wgs{i}", [128, KC * 512], BF16, stack=pst)
                   for i in range(2)]
            wus = [sb(f"wus{i}", [128, KC * 512], BF16, stack=pst)
                   for i in range(2)]
            gtmp = [sb(f"gtmp{i}", [128, 512], F32, stack=pst) for i in range(2)]
            utmp = [sb(f"utmp{i}", [128, 512], F32, stack=pst) for i in range(2)]
            fstg = [sb(f"fstg{i}", [128, R], BF16, stack=pst) for i in range(2)]

            p.wait("gpsimd", qd, n_qd_p5)
            p.wait("gpsimd", dv, n_dv_p5)
            p.wait("gpsimd", ac, n_ac_p5)
            p.wait("gpsimd", pe, n_pe_p5)

            p.wait("sync", qd, n_spill_xnt)
            for j in range(4):
                p.dma("sync", xnt2[:, j * 8192:(j + 1) * 8192],
                      xnt_scr[:, j * 8192:(j + 1) * 8192], qd)
            n_xnt2 = qd.n

            slab_free = [0, 0]
            fstg_free = [0, 0]
            psG_free = [(None, 0), (None, 0)]
            psU_free = [(None, 0), (None, 0)]
            gtmp_free = [0, 0]
            utmp_free = [0, 0]
            for fg in range(16):
                par = fg % 2
                p.wait("gpsimd", pe, slab_free[par])
                for k in range(KC):
                    p.dma("gpsimd", wgs[par][:, k * 512:(k + 1) * 512],
                          wg[k * 128:(k + 1) * 128, fg * 512:(fg + 1) * 512], qg)
                for k in range(KC):
                    p.dma("gpsimd", wus[par][:, k * 512:(k + 1) * 512],
                          wu[k * 128:(k + 1) * 128, fg * 512:(fg + 1) * 512], qg)
                n_slab = qg.n
                p.wait("tensor", qg, n_slab)
                p.wait("tensor", qd, n_xnt2)
                for fc in range(4):
                    fstg_par = fc % 2
                    for t in range(4):
                        sp = (fc * 4 + t) % 2
                        if psG_free[sp][0]:
                            p.wait("tensor", psG_free[sp][0], psG_free[sp][1])
                        for k in range(KC):
                            p.op("tensor",
                                 lambda e, sp=sp, par=par, fc=fc, t=t, k=k:
                                 e.matmul(psG[sp][:, :],
                                          wgs[par][:, k * 512 + fc * 128:
                                                   k * 512 + fc * 128 + 128],
                                          xnt2[:, k * R + t * 512:
                                               k * R + (t + 1) * 512],
                                          start=(k == 0), stop=(k == KC - 1)))
                        n_g = p.mark("tensor", pe)
                        if psU_free[sp][0]:
                            p.wait("tensor", psU_free[sp][0], psU_free[sp][1])
                        for k in range(KC):
                            p.op("tensor",
                                 lambda e, sp=sp, par=par, fc=fc, t=t, k=k:
                                 e.matmul(psU[sp][:, :],
                                          wus[par][:, k * 512 + fc * 128:
                                                   k * 512 + fc * 128 + 128],
                                          xnt2[:, k * R + t * 512:
                                               k * R + (t + 1) * 512],
                                          start=(k == 0), stop=(k == KC - 1)))
                        n_uu = p.mark("tensor", pe)
                        p.wait("scalar", pe, n_uu)
                        p.wait("scalar", dv, gtmp_free[sp])
                        p.op("scalar", lambda e, sp=sp, fg=fg, fc=fc: e.activation(
                            gtmp[sp][:, :], psG[sp][:, :], AF.Silu,
                            scale=sgt[:, fg * 4 + fc:fg * 4 + fc + 1]))
                        p.op("scalar", lambda e, sp=sp, fg=fg, fc=fc: e.activation(
                            utmp[sp][:, :], psU[sp][:, :], AF.Copy,
                            scale=sut[:, fg * 4 + fc:fg * 4 + fc + 1]))
                        n_ev = p.mark("scalar", ac)
                        psG_free[sp] = (ac, n_ev)
                        psU_free[sp] = (ac, n_ev)
                        p.wait("vector", ac, n_ev)
                        if t == 0:
                            p.wait("vector", qd, fstg_free[fstg_par])
                        p.op("vector", lambda e, sp=sp, fstg_par=fstg_par, t=t:
                             e.tensor_mul(fstg[fstg_par][:, t * 512:(t + 1) * 512],
                                          gtmp[sp][:, :], utmp[sp][:, :]))
                        n_f = p.mark("vector", dv)
                        gtmp_free[sp] = n_f
                        utmp_free[sp] = n_f
                    p.wait("sync", dv, n_f)
                    p.dma("sync", fuse_scr[(fg * 4 + fc) * 128:
                                           (fg * 4 + fc + 1) * 128, :],
                          fstg[fstg_par][:, :], qd)
                    fstg_free[fstg_par] = qd.n
                slab_free[par] = pe.n
            n_qd_p6a = qd.n
            n_pe_p6a = pe.n
            n_dv_p6a = dv.n
            n_ac_p6a = ac.n
            p.flush()

    if _lvl >= 5:
        with contextlib.ExitStack() as pst:
            psY = [pst.enter_context(nc.psum_tensor(f"psY{i}", [128, 512], F32))
                   for i in range(8)]
            fuT = sb("fuT", [128, 64 * 512], BF16, stack=pst)
            ogT = sb("ogT", [128, KC * 512], BF16, stack=pst)
            woT = sb("woT", [128, KC * 2048], BF16, stack=pst,
                     is_transpose_dest=True)
            wds = [sb(f"wds{i}", [128, 1024], BF16, stack=pst) for i in range(4)]
            xts = [sb(f"xts{i}", [128, 1024], F32, stack=pst) for i in range(4)]
            ystg = [sb(f"ystg{i}", [128, 512], F32, stack=pst) for i in range(2)]
            onb = sb("onb", [1, 128], F32, stack=pst)
            sd6 = sb("sd6", [1, D], F32, stack=pst)
            sdbb = sb("sdbb", [128, D], F32, stack=pst)
            rsbb = sb("rsbb", [128, D], BF16, stack=pst)

            psY_free = [(None, 0)] * 8

            p.wait("gpsimd", pe, n_pe_p6a)
            p.wait("gpsimd", dv, n_dv_p6a)
            p.wait("gpsimd", ac, n_ac_p6a)
            p.wait("gpsimd", qd, n_qd_p6a)

            p.op("gpsimd", lambda e: e.memset(onb[:, :], 1.0))
            n_pl6 = p.mark("gpsimd", pl)
            n_sd = p.dma("sync", sd6[:, :], sdr, qd)
            p.wait("tensor", pl, n_pl6)
            p.wait("tensor", qd, n_sd)
            prev = 0
            for j in range(4):
                p.wait("tensor", dv, prev)
                p.op("tensor", lambda e, j=j: e.matmul(
                    psY[0][:, :], onb[:, :], sd6[:, j * 512:(j + 1) * 512],
                    start=True, stop=True))
                nmm = p.mark("tensor", pe)
                p.wait("vector", pe, nmm)
                p.op("vector", lambda e, j=j: e.tensor_copy(
                    sdbb[:, j * 512:(j + 1) * 512], psY[0][:, :]))
                prev = p.mark("vector", dv)
            p.op("vector", lambda e: e.reciprocal(sd6[:, :], sd6[:, :]))
            n_dv_rs = p.mark("vector", dv)
            p.wait("tensor", dv, n_dv_rs)
            for j in range(4):
                p.wait("tensor", dv, prev)
                p.op("tensor", lambda e, j=j: e.matmul(
                    psY[0][:, :], onb[:, :], sd6[:, j * 512:(j + 1) * 512],
                    start=True, stop=True))
                nmm = p.mark("tensor", pe)
                p.wait("vector", pe, nmm)
                p.op("vector", lambda e, j=j: e.tensor_copy(
                    rsbb[:, j * 512:(j + 1) * 512], psY[0][:, :]))
                prev = p.mark("vector", dv)
            n_dv_bc = prev
            psY_free[0] = (dv, n_dv_bc)

            p.wait("sync", qg, n_qg_wd)
            p.wait("scalar", qg, n_qg_wd)
            p.wait("sync", qg, n_qg_wo)
            n_wo_k = []
            for k in range(KC):
                for fh2 in range(2):
                    p.dma("sync",
                          woT[:, k * 2048 + fh2 * 1024:
                              k * 2048 + (fh2 + 1) * 1024],
                          wo_bf[fh2 * 1024:(fh2 + 1) * 1024,
                                k * 128:(k + 1) * 128], qd, transpose=True)
                n_wo_k.append(qd.n)
            p.wait("vector", dv, n_dv_bc)
            woT_mark = []
            for k in range(KC):
                p.wait("vector", qd, n_wo_k[k])
                p.op("vector", lambda e, k=k: e.tensor_mul(
                    woT[:, k * 2048:(k + 1) * 2048],
                    woT[:, k * 2048:(k + 1) * 2048], rsbb[:, :]))
                woT_mark.append(p.mark("vector", dv))

            fu_free = 0
            og_free = 0
            wds_free = [(None, 0)] * 4
            xts_free = [0, 0, 0, 0]
            ystg_free = [0, 0]
            ld_done = False

            for tt in range(4):
                p.wait("sync", pe, og_free)
                p.wait("sync", qd, n_qd_p5)
                for k in range(KC):
                    for hf in range(2):
                        b0 = 2 * tt + hf
                        p.dma("sync",
                              ogT[:, k * 512 + hf * 256:k * 512 + hf * 256 + 256],
                              og_scr[b0 * 128:(b0 + 1) * 128,
                                     k * BL:(k + 1) * BL], qd)
                n_ld_og = qd.n
                p.wait("scalar", pe, fu_free)
                p.wait("scalar", qd, n_qd_p6a)
                for k in range(64):
                    p.dma("scalar", fuT[:, k * 512:(k + 1) * 512],
                          fuse_scr[k * 128:(k + 1) * 128,
                                   tt * 512:(tt + 1) * 512], qs)
                n_ld_fu = qs.n

                for fh in range(2):
                    for tsub in range(4):
                        p.wait("sync", dv, xts_free[tsub])
                        p.dma("sync", xts[tsub][:, :],
                              x[tt * 512 + tsub * 128:
                                tt * 512 + (tsub + 1) * 128,
                                fh * 1024:(fh + 1) * 1024], qd)
                    n_ld_x = qd.n
                    p.wait("tensor", qd, n_ld_og)
                    first = True
                    for k in range(KC):
                        p.wait("tensor", dv, woT_mark[k])
                        for tsub in range(4):
                            for foc in range(2):
                                sp = tsub * 2 + foc
                                if first and psY_free[sp][0]:
                                    p.wait("tensor", psY_free[sp][0],
                                           psY_free[sp][1])
                                p.op("tensor",
                                     lambda e, k=k, tsub=tsub, foc=foc, sp=sp,
                                     fr=first, fh=fh:
                                     e.matmul(psY[sp][:, :],
                                              ogT[:, k * 512 + tsub * 128:
                                                  k * 512 + tsub * 128 + 128],
                                              woT[:, k * 2048 + fh * 1024 +
                                                  foc * 512:
                                                  k * 2048 + fh * 1024 +
                                                  (foc + 1) * 512],
                                              start=fr, stop=False))
                        first = False
                    p.wait("tensor", qs, n_ld_fu)
                    for k in range(64):
                        kp = k % 4
                        ldeng, ldsem = (("sync", qd) if k % 2 == 0
                                        else ("scalar", qs))
                        if wds_free[kp][0]:
                            p.wait(ldeng, wds_free[kp][0], wds_free[kp][1])
                        n_wd_ld = p.dma(ldeng, wds[kp][:, :],
                                        wd_bf[k * 128:(k + 1) * 128,
                                              fh * 1024:(fh + 1) * 1024],
                                        ldsem)
                        p.wait("tensor", ldsem, n_wd_ld)
                        if k == 63:
                            sp_marks = {}
                        for tsub in range(4):
                            for foc in range(2):
                                sp = tsub * 2 + foc
                                p.op("tensor",
                                     lambda e, k=k, kp=kp, tsub=tsub, foc=foc,
                                     sp=sp, last=(k == 63):
                                     e.matmul(psY[sp][:, :],
                                              fuT[:, k * 512 + tsub * 128:
                                                  k * 512 + tsub * 128 + 128],
                                              wds[kp][:, foc * 512:
                                                      (foc + 1) * 512],
                                              start=False, stop=last))
                                if k == 63:
                                    sp_marks[sp] = p.mark("tensor", pe)
                        n_wd_mm = p.mark("tensor", pe)
                        wds_free[kp] = (pe, n_wd_mm)
                    n_fh_mm = p.mark("tensor", pe)
                    for tsub in range(4):
                        for foc in range(2):
                            sp = tsub * 2 + foc
                            yb = sp % 2
                            p.wait("vector", pe, sp_marks[sp])
                            p.wait("vector", qd, n_ld_x)
                            p.wait("vector", qg, ystg_free[yb])
                            p.op("vector",
                                 lambda e, sp=sp, yb=yb, fh=fh, foc=foc:
                                 e.tensor_mul(ystg[yb][:, :], psY[sp][:, :],
                                              sdbb[:, fh * 1024 + foc * 512:
                                                   fh * 1024 +
                                                   (foc + 1) * 512]))
                            n_ev1 = p.mark("vector", dv)
                            psY_free[sp] = (dv, n_ev1)
                            p.op("vector",
                                 lambda e, yb=yb, tsub=tsub, foc=foc:
                                 e.tensor_add(ystg[yb][:, :], ystg[yb][:, :],
                                              xts[tsub][:, foc * 512:
                                                        (foc + 1) * 512]))
                            n_ev2 = p.mark("vector", dv)
                            if foc == 1:
                                xts_free[tsub] = n_ev2
                            p.wait("gpsimd", dv, n_ev2)
                            p.dma("gpsimd",
                                  y[tt * 512 + tsub * 128:
                                    tt * 512 + (tsub + 1) * 128,
                                    fh * 1024 + foc * 512:
                                    fh * 1024 + (foc + 1) * 512],
                                  ystg[yb][:, :], qg)
                            ystg_free[yb] = qg.n
                fu_free = pe.n
                og_free = pe.n
            p.wait("sync", qd, qd.n)
            p.wait("sync", qg, qg.n)
            p.flush()

    if _lvl < 5:
        p.wait("sync", qd, qd.n)
        p.wait("sync", qg, qg.n)
        p.wait("sync", qs, qs.n)
        p.wait("sync", cc, cc.n)
        p.wait("vector", dv, dv.n)
        p.wait("scalar", ac, ac.n)
        p.wait("tensor", pe, pe.n)
        p.wait("gpsimd", pl, pl.n)
        p.flush()
    st.close()
    return nc


_NC_CACHE = {}


def _get_nc():
    if "nc" not in _NC_CACHE:
        _NC_CACHE["nc"] = build_nc()
    return _NC_CACHE["nc"]


def make_in_maps(L0_attn_x, L0_attn_slope_rate, L0_attn_W_qkvu,
                 L0_attn_W_out, L0_ffn_w_gate, L0_ffn_w_up, L0_ffn_w_down,
                 L0_ffn_s_gate, L0_ffn_s_up, L0_ffn_s_down):
    xf = np.ascontiguousarray(np.asarray(L0_attn_x).reshape(NCORES * R, D),
                              dtype=np.float32)
    slope = np.ascontiguousarray(np.asarray(L0_attn_slope_rate).reshape(H),
                                 dtype=np.float32)
    wq = np.ascontiguousarray(np.asarray(L0_attn_W_qkvu), dtype=np.float32)
    wo = np.ascontiguousarray(np.asarray(L0_attn_W_out), dtype=np.float32)
    wg_ = np.ascontiguousarray(np.asarray(L0_ffn_w_gate), dtype=np.int8)
    wu_ = np.ascontiguousarray(np.asarray(L0_ffn_w_up), dtype=np.int8)
    wd_ = np.ascontiguousarray(np.asarray(L0_ffn_w_down), dtype=np.int8)
    sg_ = np.ascontiguousarray(np.asarray(L0_ffn_s_gate).reshape(FI // 128,
                                                                 128),
                               dtype=np.float32)
    su_ = np.ascontiguousarray(np.asarray(L0_ffn_s_up).reshape(FI // 128,
                                                               128),
                               dtype=np.float32)
    sd_ = np.ascontiguousarray(np.asarray(L0_ffn_s_down).reshape(1, D),
                               dtype=np.float32)

    in_maps = []
    for c in range(NCORES):
        pm = np.array([[1.0 if r < c else 0.0 for r in range(NCORES)]],
                      dtype=np.float32)
        pd = np.array([[float(c - 1 - r) if r < c else 0.0
                        for r in range(NCORES)]], dtype=np.float32)
        in_maps.append({
            "x": np.ascontiguousarray(xf[c * R:(c + 1) * R]),
            "slope": slope, "wqkvu": wq, "wout": wo,
            "wg": wg_, "wu": wu_, "wd": wd_,
            "sg": sg_, "su": su_, "sdr": sd_,
            "pmask": pm, "pdist": pd,
        })
    return in_maps


def kernel(**inputs):
    in_maps = make_in_maps(**inputs)
    nc = _get_nc()
    res = run_bass_kernel_spmd(nc, in_maps, core_ids=list(range(NCORES)))
    out = np.concatenate([np.asarray(r["y"]) for r in res.results], axis=0)
    return out.reshape(1, NCORES * R, D).astype(np.float32)


# version 13
# speedup vs baseline: 1.0222x; 1.0209x over previous
import contextlib
import numpy as np

import concourse.bass as bass
import concourse.mybir as mybir
from concourse.bass_utils import run_bass_kernel_spmd

F32 = mybir.dt.float32
BF16 = mybir.dt.bfloat16
I8 = mybir.dt.int8
AF = mybir.ActivationFunctionType
ALU = mybir.AluOpType
AXL = mybir.AxisListType

NCORES = 8
R = 2048
D = 2048
H = 16
HD = 128
B = 8
BL = 256
FI = 8192
KC = D // 128
EPS_LN = 1e-5
EPS_RMS = 1e-6

ENGS = ("sync", "scalar", "vector", "tensor", "gpsimd")


class SemC:
    def __init__(self, h):
        self.h = h
        self.n = 0


class Plan:

    def __init__(self, nc, sems):
        self.nc = nc
        self.sems = sems
        self.ops = {e: [] for e in ENGS}

    def wait(self, eng, sem, val):
        if val is None or val <= 0:
            return
        self.ops[eng].append(("w", sem.h, val))

    def dma(self, eng, out, in_, sem, transpose=False):
        sem.n += 16

        def fn(e, o=out, i=in_, tp=transpose):
            return e.dma_start(out=o, in_=i, transpose=tp)
        self.ops[eng].append(["i", fn, [(sem.h, 16)]])
        return sem.n

    def op(self, eng, fn, sem=None, k=1):
        incs = []
        v = None
        if sem is not None:
            sem.n += k
            v = sem.n
            incs.append((sem.h, k))
        self.ops[eng].append(["i", fn, incs])
        return v

    def mark(self, eng, sem, k=1):
        entry = None
        for e_ in reversed(self.ops[eng]):
            if e_[0] == "i":
                entry = e_
                break
        assert entry is not None, f"mark on empty stream {eng}"
        sem.n += k
        for j, (sh, kk) in enumerate(entry[2]):
            if sh is sem.h:
                entry[2][j] = (sh, kk + k)
                break
        else:
            entry[2].append((sem.h, k))
        return sem.n

    def flush(self):
        with self.nc.Block() as blk:
            for eng in ENGS:
                ops = self.ops[eng]
                if not ops:
                    continue

                def runner(e, _ops=tuple(tuple(o) for o in ops)):
                    for o in _ops:
                        if o[0] == "w":
                            e.wait_ge(o[1], o[2])
                        else:
                            inst = o[1](e)
                            for (s, k) in o[2]:
                                inst.then_inc(s, k)
                getattr(blk, eng)(runner)
        self.ops = {e: [] for e in ENGS}


def build_nc(debug=(), stop_after="full"):
    _lvl = {"p1": 1, "p2": 2, "p35": 3, "p6a": 4, "full": 5}[stop_after]
    nc = bass.Bass("TRN2", target_bir_lowering=False)
    dbg = set(debug)

    def scratch(name, shape, dtype):
        kind = "ExternalOutput" if name in dbg else "Internal"
        return nc.dram_tensor(name, shape, dtype, kind=kind).ap()

    x = nc.dram_tensor("x", [R, D], F32, kind="ExternalInput").ap()
    slope = nc.dram_tensor("slope", [H], F32, kind="ExternalInput").ap()
    wqkvu = nc.dram_tensor("wqkvu", [4 * D, D], F32, kind="ExternalInput").ap()
    wout = nc.dram_tensor("wout", [D, D], F32, kind="ExternalInput").ap()
    wg = nc.dram_tensor("wg", [D, FI], I8, kind="ExternalInput").ap()
    wu = nc.dram_tensor("wu", [D, FI], I8, kind="ExternalInput").ap()
    wd = nc.dram_tensor("wd", [FI, D], I8, kind="ExternalInput").ap()
    sg = nc.dram_tensor("sg", [FI // 128, 128], F32, kind="ExternalInput").ap()
    su = nc.dram_tensor("su", [FI // 128, 128], F32, kind="ExternalInput").ap()
    sdr = nc.dram_tensor("sdr", [1, D], F32, kind="ExternalInput").ap()
    pmask = nc.dram_tensor("pmask", [1, NCORES], F32, kind="ExternalInput").ap()
    pdist = nc.dram_tensor("pdist", [1, NCORES], F32, kind="ExternalInput").ap()
    y = nc.dram_tensor("y", [R, D], F32, kind="ExternalOutput").ap()

    xnt_scr = scratch("xnt_scr", [128, KC * R], BF16)
    wq_bf = scratch("wq_bf", [4 * D, D], BF16)
    wo_bf = scratch("wo_bf", [D, D], BF16)
    wd_bf = scratch("wd_bf", [FI, D], BF16)
    q_scr = scratch("q_scr", [B * 128, H * BL], BF16)
    k_scr = scratch("k_scr", [D, R], BF16)
    k_img = scratch("k_img", [B * 128, H * BL], BF16)
    v_scr = scratch("v_scr", [D, R], BF16)
    u_scr = scratch("u_scr", [B * 128, H * BL], F32)
    og_scr = scratch("og_scr", [B * 128, H * BL], BF16)
    fuse_scr = scratch("fuse_scr", [FI, R], BF16)
    cc_in = scratch("cc_in", [128, H * HD], F32)
    cc_out = scratch("cc_out", [NCORES * 128, H * HD], F32)
    wcb = scratch("wcb", [H, NCORES], F32)
    cc_dbg = scratch("cc_dbg", [128, H * HD], F32)
    sin_dbg = scratch("sin_dbg", [128, H * HD], F32)
    wcomb_dbg = scratch("wcomb_dbg", [128, 128], F32)

    st = contextlib.ExitStack()
    sems = {nm: SemC(st.enter_context(nc.semaphore(nm)))
            for nm in ("qd", "qs", "qg", "pe", "dv", "ac", "pl", "cc")}
    qd, qs, qg = sems["qd"], sems["qs"], sems["qg"]
    pe, dv, ac, pl, cc = (sems["pe"], sems["dv"], sems["ac"], sems["pl"],
                          sems["cc"])

    def sb(name, shape, dtype, stack=st, **kw):
        return stack.enter_context(nc.sbuf_tensor(name, shape, dtype, **kw))

    sgt = sb("sgt", [128, FI // 128], F32)
    sut = sb("sut", [128, FI // 128], F32)

    mid_st = contextlib.ExitStack()
    sbc = sb("sbc", [128, H], F32, stack=mid_st)
    nsbc = sb("nsbc", [128, H], F32, stack=mid_st)
    bdp = sb("bdp", [128, B * H], F32, stack=mid_st)
    iotam = sb("iotam", [128, BL], F32, stack=mid_st)
    idxp = sb("idxp", [128, 2 * BL], F32, stack=mid_st)
    ceps5 = sb("ceps5", [128, 1], F32, stack=mid_st)
    ceps6 = sb("ceps6", [128, 1], F32, stack=mid_st)
    on1 = sb("on1", [1, 128], F32, stack=mid_st)
    onP = sb("onP", [128, 1], BF16, stack=mid_st)
    wcomb = sb("wcomb", [128, 128], F32, stack=mid_st)
    tcol = sb("tcol", [128, 1], F32, stack=mid_st)
    tv = sb("tv", [128, 2], F32, stack=mid_st)
    otile = sb("otile", [128, HD], F32, stack=mid_st)
    Sin = sb("Sin", [128, H * HD], F32, stack=mid_st)
    Sinb = sb("Sinb", [128, H * HD], BF16, stack=mid_st)
    snap = sb("snap", [128, (B - 1) * H * HD], BF16, stack=mid_st)

    p = Plan(nc, sems)

    s_order = [4, 5, 6, 7, 8, 9, 10, 11, 0, 1, 2, 3, 12, 13, 14, 15]

    with contextlib.ExitStack() as pst:
        psq = [pst.enter_context(nc.psum_tensor(f"psq{i}", [128, 512], F32))
               for i in range(4)]
        psT = [pst.enter_context(nc.psum_tensor(f"psT{i}", [128, 1024], BF16))
               for i in range(2)]
        psS = [pst.enter_context(nc.psum_tensor(f"psS{i}", [128, 512], F32))
               for i in range(2)]
        id128 = sb("id128", [128, 128], F32, stack=pst)
        id128b = sb("id128b", [128, 128], BF16, stack=pst)
        bdvec = sb("bdvec", [128, H], F32, stack=pst)
        stv = sb("stv", [128, 2 * H], F32, stack=pst)
        xnt = sb("xnt", [128, KC * R], BF16, stack=pst)
        wts = [sb(f"wts{i}", [128, KC * 512], BF16, stack=pst,
                  is_transpose_dest=True) for i in range(2)]
        stg = [sb(f"stg{i}", [128, R], BF16, stack=pst) for i in range(2)]
        ustg = sb("ustg", [128, R], F32, stack=pst)

        p.op("gpsimd", lambda e: e.memset(ceps5[:, :], EPS_LN))
        p.op("gpsimd", lambda e: e.memset(ceps6[:, :], EPS_RMS))
        p.op("gpsimd", lambda e: e.memset(on1[:, :], 1.0))
        p.op("gpsimd", lambda e: e.memset(onP[:, :], 1.0))
        p.op("gpsimd", lambda e: e.memset(otile[:, :], 1.0))
        p.op("gpsimd", lambda e: e.memset(id128[:, :], 1.0))
        p.op("gpsimd", lambda e: e.iota(iotam[:, :], [[1, BL]], base=0,
                                        channel_multiplier=0,
                                        allow_small_or_imprecise_dtypes=True))
        p.op("gpsimd", lambda e: e.iota(tcol[:, :], [[1, 1]], base=0,
                                        channel_multiplier=1,
                                        allow_small_or_imprecise_dtypes=True))
        p.op("gpsimd", lambda e: e.affine_select(
            id128[:, :], id128[:, :], [[1, 128]], ALU.is_equal, 0.0,
            base=0, channel_multiplier=-1))
        for c in range(2):
            p.op("gpsimd", lambda e, c=c: e.iota(
                idxp[:, c * BL:(c + 1) * BL], [[1, BL]], base=-(c * 128),
                channel_multiplier=-1, allow_small_or_imprecise_dtypes=True))
            p.op("gpsimd", lambda e, c=c: e.affine_select(
                idxp[:, c * BL:(c + 1) * BL], idxp[:, c * BL:(c + 1) * BL],
                [[1, BL]], ALU.is_ge, 1.0e30,
                base=-(c * 128), channel_multiplier=-1))
        n_pl0 = p.mark("gpsimd", pl)

        n_qg_wq = {}
        for s_ in s_order[:2]:
            n_qg_wq[s_] = p.dma("gpsimd",
                                wq_bf[s_ * 512:(s_ + 1) * 512, :],
                                wqkvu[s_ * 512:(s_ + 1) * 512, :], qg)

        with contextlib.ExitStack() as p01:
            srow = sb("srow", [1, H], F32, stack=p01)
            s16 = sb("s16", [H, 1], F32, stack=p01)
            pmr = sb("pmr", [1, NCORES], F32, stack=p01)
            pdr = sb("pdr", [1, NCORES], F32, stack=p01)
            sg64 = sb("sg64", [64, 128], F32, stack=p01)
            su64 = sb("su64", [64, 128], F32, stack=p01)
            w16 = sb("w16", [H, NCORES], F32, stack=p01)
            wrow = sb("wrow", [1, H * NCORES], F32, stack=p01)
            xt = [sb(f"xt{i}", [128, D], F32, stack=p01) for i in range(3)]
            xnb = [sb(f"xnb{i}", [128, D], BF16, stack=p01) for i in range(3)]
            stat = [sb(f"stat{i}", [128, 8], F32, stack=p01) for i in range(3)]

            p.dma("sync", srow[:, :], slope.rearrange("(o h) -> o h", o=1), qd)
            p.dma("sync", s16[:, :], slope.rearrange("(h o) -> h o", o=1), qd)
            p.dma("sync", pmr[:, :], pmask, qd)
            p.dma("sync", pdr[:, :], pdist, qd)
            p.dma("sync", sg64[:, :], sg, qd)
            n_qd_consts = p.dma("sync", su64[:, :], su, qd)

            n_wslab01 = []
            for pp in range(2):
                s0 = s_order[pp]
                p.wait("sync", qg, n_qg_wq[s0])
                for k in range(KC):
                    p.dma("sync", wts[pp][:, k * 512:(k + 1) * 512],
                          wq_bf[s0 * 512:(s0 + 1) * 512,
                                k * 128:(k + 1) * 128], qd, transpose=True)
                n_wslab01.append(qd.n)

            psA, psB = psq[0], psq[1]
            p.wait("tensor", qd, n_qd_consts)
            p.wait("tensor", pl, n_pl0)
            p.op("tensor", lambda e: e.matmul(psA[:, 0:16], on1[:, :],
                                              srow[:, :], start=True, stop=True))
            p.op("tensor", lambda e: e.matmul(psA[:, 16:24], on1[:, :],
                                              pmr[:, :], start=True, stop=True))
            p.op("tensor", lambda e: e.matmul(psA[:, 24:32], on1[:, :],
                                              pdr[:, :], start=True, stop=True))
            p.op("tensor", lambda e: e.transpose(psA[:, 32:96], sg64[:, :],
                                                 id128[0:64, 0:64]))
            p.op("tensor", lambda e: e.transpose(psA[:, 96:160], su64[:, :],
                                                 id128[0:64, 0:64]))
            n_pe0 = p.mark("tensor", pe)

            p.wait("vector", pe, n_pe0)
            p.op("vector", lambda e: e.tensor_copy(sbc[:, :], psA[:, 0:16]))
            p.op("vector", lambda e: e.tensor_scalar_mul(nsbc[:, :],
                                                         psA[:, 0:16], -1.0))
            p.op("vector", lambda e: e.tensor_copy(sgt[:, :], psA[:, 32:96]))
            p.op("vector", lambda e: e.tensor_copy(sut[:, :], psA[:, 96:160]))
            p.op("vector", lambda e: e.tensor_scalar_mul(w16[:, :],
                                                         psA[0:16, 24:32],
                                                         s16[:, :]))
            p.op("vector", lambda e: e.tensor_copy(id128b[:, :], id128[:, :]))
            p.wait("vector", pl, n_pl0)
            for c in range(2):
                p.op("vector", lambda e, c=c: e.tensor_scalar_add(
                    tv[:, c:c + 1], tcol[:, :], float(128 * c - 255)))
            for c in range(2):
                for h in range(H):
                    p.op("vector", lambda e, c=c, h=h: e.tensor_mul(
                        stv[:, c * H + h:c * H + h + 1], tv[:, c:c + 1],
                        sbc[:, h:h + 1]))
            p.op("vector", lambda e: e.memset(Sin[:, :], 0.0))
            n_dv_ev0 = p.mark("vector", dv)

            p.wait("scalar", dv, n_dv_ev0)
            p.op("scalar", lambda e: e.activation(bdvec[:, :], sbc[:, :],
                                                  AF.Exp, scale=-256.0))
            for b_ in range(B):
                p.op("scalar", lambda e, b_=b_: e.activation(
                    bdp[:, b_ * H:(b_ + 1) * H], sbc[:, :], AF.Exp,
                    scale=-256.0 * b_))
            p.op("scalar", lambda e: e.activation(w16[:, :], w16[:, :],
                                                  AF.Exp, scale=-2048.0))
            n_ac_w16 = p.mark("scalar", ac)
            p.wait("vector", ac, n_ac_w16)
            p.op("vector", lambda e: e.tensor_mul(w16[:, :], w16[:, :],
                                                  psA[0:16, 16:24]))
            n_dv_w16 = p.mark("vector", dv)

            p.wait("sync", dv, n_dv_w16)
            nb1 = p.dma("sync", wcb, w16[:, :], qd)
            p.wait("sync", qd, nb1)
            nb2 = p.dma("sync", wrow[:, :],
                        wcb.rearrange("h r -> (h r)").rearrange(
                            "(o n) -> o n", o=1), qd)
            p.wait("tensor", qd, nb2)
            p.op("tensor", lambda e: e.matmul(psB[:, 0:128], on1[:, :],
                                              wrow[:, :], start=True, stop=True))
            n_pe_wc = p.mark("tensor", pe)
            p.wait("vector", pe, n_pe_wc)
            p.op("vector", lambda e: e.tensor_copy(wcomb[:, :], psB[:, 0:128]))
            n_dv_wc = p.mark("vector", dv)
            psq_init_free = [(dv, n_dv_w16), (dv, n_dv_wc), (None, 0), (None, 0)]

            xt_free = [(None, 0)] * 3
            xt_free2 = [(None, 0)] * 3
            xn_free_pe = [0, 0, 0]
            stat_free = [(None, 0)] * 3
            psT_free = [(None, 0), (None, 0)]
            for i in range(16):
                par = i % 3
                eng, qsem = "scalar", qs
                if xt_free[par][0]:
                    p.wait(eng, xt_free[par][0], xt_free[par][1])
                if xt_free2[par][0]:
                    p.wait(eng, xt_free2[par][0], xt_free2[par][1])
                n_ld = p.dma(eng, xt[par][:, :], x[i * 128:(i + 1) * 128, :],
                             qsem)
                p.wait("vector", qsem, n_ld)
                p.op("vector", lambda e, par=par: e.reduce_sum(
                    stat[par][:, 0:1], xt[par][:, :], axis=AXL.X))
                n_sum = p.mark("vector", dv)
                p.wait("scalar", qsem, n_ld)
                if stat_free[par][0]:
                    p.wait("scalar", stat_free[par][0], stat_free[par][1])
                p.op("scalar", lambda e, par=par: e.activation(
                    ustg[:, :], xt[par][:, :], AF.Square,
                    accum_out=stat[par][:, 2:3]))
                p.wait("scalar", dv, n_sum)
                p.op("scalar", lambda e, par=par: e.mul(
                    stat[par][:, 1:2], stat[par][:, 0:1], 1.0 / D))
                p.op("scalar", lambda e, par=par: e.activation(
                    stat[par][:, 3:4], stat[par][:, 0:1], AF.Square))
                n_ss = p.mark("scalar", ac)
                p.wait("vector", ac, n_ss)
                p.op("vector", lambda e, par=par: e.scalar_tensor_tensor(
                    stat[par][:, 4:5], stat[par][:, 3:4], -1.0 / D,
                    stat[par][:, 2:3], ALU.mult, ALU.add))
                n_var = p.mark("vector", dv)
                p.wait("scalar", dv, n_var)
                p.op("scalar", lambda e, par=par: e.activation(
                    stat[par][:, 5:6], stat[par][:, 4:5], AF.Ln, scale=1.0 / D,
                    bias=ceps5[:, :]))
                p.op("scalar", lambda e, par=par: e.activation(
                    stat[par][:, 6:7], stat[par][:, 5:6], AF.Exp, scale=-0.5))
                n_rstd = p.mark("scalar", ac)
                p.wait("vector", ac, n_rstd)
                p.wait("vector", pe, xn_free_pe[par])
                p.op("vector", lambda e, par=par: e.tensor_scalar(
                    xnb[par][:, :], xt[par][:, :], stat[par][:, 1:2],
                    stat[par][:, 6:7], ALU.subtract, ALU.mult))
                n_xn = p.mark("vector", dv)
                xt_free[par] = (dv, n_xn)
                xt_free2[par] = (ac, n_ss)
                stat_free[par] = (dv, n_xn)
                p.wait("tensor", dv, n_xn)
                for half in range(2):
                    tb = half
                    if psT_free[tb][0]:
                        p.wait("tensor", psT_free[tb][0], psT_free[tb][1])
                    for k in range(8):
                        kk = half * 8 + k
                        p.op("tensor", lambda e, par=par, tb=tb, k=k, kk=kk:
                             e.transpose(psT[tb][:, k * 128:(k + 1) * 128],
                                         xnb[par][:, kk * 128:(kk + 1) * 128],
                                         id128b[:, :]))
                    n_tp = p.mark("tensor", pe)
                    p.wait("vector", pe, n_tp)
                    for k in range(8):
                        kk = half * 8 + k
                        p.op("vector", lambda e, tb=tb, k=k, kk=kk, i=i:
                             e.tensor_copy(
                                 xnt[:, kk * R + i * 128:kk * R + i * 128 + 128],
                                 psT[tb][:, k * 128:(k + 1) * 128]))
                    n_ev = p.mark("vector", dv)
                    psT_free[tb] = (dv, n_ev)
                xn_free_pe[par] = pe.n
            n_dv_xnt = dv.n

        ktok = [sb(f"ktok{i}", [128, 2 * D], BF16, stack=pst,
                   is_transpose_dest=True) for i in range(1)]
        vtok3 = [sb(f"vtok3{i}", [128, 2 * D], BF16, stack=pst,
                    is_transpose_dest=True) for i in range(1)]
        kd_bc = [sb(f"kdbc{c}", [128, D], BF16, stack=pst) for c in range(2)]
        S = sb("S", [128, H * HD], F32, stack=pst)

        for c in range(2):
            for h in range(H):
                p.op("scalar", lambda e, c=c, h=h: e.activation(
                    kd_bc[c][:, h * HD:(h + 1) * HD], otile[:, :], AF.Exp,
                    scale=stv[:, c * H + h:c * H + h + 1]))
        n_ac_kd = p.mark("scalar", ac)
        p.wait("vector", ac, n_ac_kd)
        p.op("vector", lambda e: e.memset(S[:, :], 0.0))
        n_dv_s0 = p.mark("vector", dv)

        wts_free = [0, 0]
        stg_free = [None, None]
        ustg_free = None
        psum_free = list(psq_init_free)
        slot_i = 0
        n_qs_kdone = None
        n_qs_vdone = None
        p3_tok_free = 0
        p3_ps_free = [(None, 0), (None, 0)]
        p3_upd_prev = n_dv_s0
        p3_n_tok = 0
        n_qd_cc = None

        def emit_p3_block(b_):
            nonlocal_vals = None

        p3_state = {"tok_free": [0], "upd_prev": n_dv_s0,
                    "ps_free": [(None, 0), (None, 0)]}

        def p3_block(b_):
            st3 = p3_state
            bp = 0
            kt, vt = ktok[bp], vtok3[bp]
            p.wait("sync", pe, st3["tok_free"][bp])
            p.wait("sync", qs, n_qs_vdone)
            for c in range(2):
                p.dma("sync", kt[:, c * D:(c + 1) * D],
                      k_scr[:, b_ * BL + c * 128:b_ * BL + (c + 1) * 128],
                      qd, transpose=True)
                p.dma("sync", vt[:, c * D:(c + 1) * D],
                      v_scr[:, b_ * BL + c * 128:b_ * BL + (c + 1) * 128],
                      qd, transpose=True)
            n_tok = qd.n
            p.wait("vector", qd, n_tok)
            p.wait("vector", pe, st3["tok_free"][bp])
            for c in range(2):
                p.op("vector", lambda e, c=c, vt=vt: e.tensor_mul(
                    vt[:, c * D:(c + 1) * D], vt[:, c * D:(c + 1) * D],
                    kd_bc[c][:, :]))
            if b_ >= 1:
                p.op("vector", lambda e, b_=b_: e.tensor_copy(
                    snap[:, (b_ - 1) * H * HD:b_ * H * HD], S[:, :]))
            n_ksc = p.mark("vector", dv)
            for h in range(H):
                hp = h % 2
                p.wait("tensor", dv, n_ksc)
                if st3["ps_free"][hp][0]:
                    p.wait("tensor", st3["ps_free"][hp][0],
                           st3["ps_free"][hp][1])
                for c in range(2):
                    p.op("tensor", lambda e, hp=hp, c=c, h=h, kt=kt, vt=vt:
                         e.matmul(psS[hp][:, 0:HD],
                                  kt[:, c * D + h * HD:c * D + (h + 1) * HD],
                                  vt[:, c * D + h * HD:c * D + (h + 1) * HD],
                                  start=(c == 0), stop=(c == 1)))
                n_mm = p.mark("tensor", pe)
                p.wait("vector", pe, n_mm)
                p.op("vector", lambda e, hp=hp, h=h: e.scalar_tensor_tensor(
                    S[:, h * HD:(h + 1) * HD], S[:, h * HD:(h + 1) * HD],
                    bdvec[:, h:h + 1], psS[hp][:, 0:HD], ALU.mult, ALU.add))
                n_upd = p.mark("vector", dv)
                st3["ps_free"][hp] = (dv, n_upd)
                st3["upd_prev"] = n_upd
            st3["tok_free"][bp] = pe.n

        n_spill_xnt = 0
        wts_loaded = {0: n_wslab01[0], 1: n_wslab01[1]}

        def emit_wts_loads(pp):
            s2_ = s_order[pp]
            par_ = pp % 2
            p.wait("sync", qg, n_qg_wq[s2_])
            p.wait("sync", pe, wts_free[par_])
            for k in range(KC):
                p.dma("sync", wts[par_][:, k * 512:(k + 1) * 512],
                      wq_bf[s2_ * 512:(s2_ + 1) * 512,
                            k * 128:(k + 1) * 128],
                      qd, transpose=True)
            wts_loaded[pp] = qd.n

        for pos, s_ in enumerate(s_order):
            par = pos % 2
            kind = ("q", "k", "v", "u")[s_ // 4]
            chunk0 = (s_ % 4) * 4
            n_wslab = wts_loaded[pos]
            p.wait("tensor", qd, n_wslab)
            p.wait("tensor", dv, n_dv_xnt)
            for fc in range(4):
                chunk = chunk0 + fc
                ev_marks = []
                for t in range(4):
                    slot = slot_i % 4
                    slot_i += 1
                    if psum_free[slot][0]:
                        p.wait("tensor", psum_free[slot][0],
                               psum_free[slot][1])
                    for k in range(KC):
                        p.op("tensor",
                             lambda e, slot=slot, par=par, fc=fc, t=t, k=k:
                             e.matmul(psq[slot][:, :],
                                      wts[par][:, k * 512 + fc * 128:
                                               k * 512 + fc * 128 + 128],
                                      xnt[:, k * R + t * 512:
                                          k * R + (t + 1) * 512],
                                      start=(k == 0), stop=(k == KC - 1)))
                    n_mm = p.mark("tensor", pe)
                    if kind in ("q", "k"):
                        p.wait("scalar", pe, n_mm)
                        if t == 0 and stg_free[par]:
                            p.wait("scalar", qs, stg_free[par][1])
                        p.op("scalar",
                             lambda e, slot=slot, par=par, t=t: e.activation(
                                 stg[par][:, t * 512:(t + 1) * 512],
                                 psq[slot][:, :], AF.Silu))
                        nev = p.mark("scalar", ac)
                        psum_free[slot] = (ac, nev)
                    else:
                        p.wait("vector", pe, n_mm)
                        if t == 0:
                            fr = stg_free[par] if kind == "v" else ustg_free
                            if fr:
                                p.wait("vector", qs, fr[1])
                        if kind == "v":
                            p.op("vector",
                                 lambda e, slot=slot, par=par, t=t:
                                 e.tensor_copy(
                                     stg[par][:, t * 512:(t + 1) * 512],
                                     psq[slot][:, :]))
                        else:
                            p.op("vector",
                                 lambda e, slot=slot, t=t:
                                 e.tensor_copy(
                                     ustg[:, t * 512:(t + 1) * 512],
                                     psq[slot][:, :]))
                        nev = p.mark("vector", dv)
                        psum_free[slot] = (dv, nev)
                    ev_marks.append(psum_free[slot])
                last = ev_marks[-1]
                p.wait("scalar", last[0], last[1])
                if kind == "q":
                    for b_ in range(B):
                        p.dma("scalar",
                              q_scr[b_ * 128:(b_ + 1) * 128,
                                    chunk * BL:(chunk + 1) * BL],
                              stg[par][:, b_ * BL:(b_ + 1) * BL], qs)
                    stg_free[par] = ("qs", qs.n)
                elif kind == "k":
                    p.dma("scalar", k_scr[chunk * 128:(chunk + 1) * 128, :],
                          stg[par][:, :], qs)
                    for b_ in range(B):
                        p.dma("scalar",
                              k_img[b_ * 128:(b_ + 1) * 128,
                                    chunk * BL:(chunk + 1) * BL],
                              stg[par][:, b_ * BL:(b_ + 1) * BL], qs)
                    stg_free[par] = ("qs", qs.n)
                elif kind == "v":
                    p.dma("scalar", v_scr[chunk * 128:(chunk + 1) * 128, :],
                          stg[par][:, :], qs)
                    stg_free[par] = ("qs", qs.n)
                else:
                    for b_ in range(B):
                        p.dma("scalar",
                              u_scr[b_ * 128:(b_ + 1) * 128,
                                    chunk * BL:(chunk + 1) * BL],
                              ustg[:, b_ * BL:(b_ + 1) * BL], qs)
                    ustg_free = ("qs", qs.n)
            wts_free[par] = pe.n
            if 2 <= pos + 1 < 16:
                emit_wts_loads(pos + 1)
            if pos + 2 < 16:
                s2 = s_order[pos + 2]
                p.wait("gpsimd", qd, n_wslab)
                n_qg_wq[s2] = p.dma("gpsimd",
                                    wq_bf[s2 * 512:(s2 + 1) * 512, :],
                                    wqkvu[s2 * 512:(s2 + 1) * 512, :], qg)
            if pos == 3:
                n_qs_kdone = qs.n
            if pos == 7:
                n_qs_vdone = qs.n
            if pos == 8:
                p.wait("sync", dv, n_dv_xnt)
                for j in range(4):
                    p.dma("sync", xnt_scr[:, j * 8192:(j + 1) * 8192],
                          xnt[:, j * 8192:(j + 1) * 8192], qd)
                n_spill_xnt = qd.n
            if 8 <= pos <= 11:
                p3_block(2 * (pos - 8))
                p3_block(2 * (pos - 8) + 1)
            if pos == 11:
                p.wait("sync", dv, p3_state["upd_prev"])
                p.dma("sync", cc_in, S[:, :], qd)
                n_qd_cc = qd.n
                if "cc_dbg" in dbg:
                    p.dma("sync", cc_dbg, S[:, :], qd)
                    p.wait("sync", qd, qd.n)
                p.wait("gpsimd", qd, n_qd_cc)
                p.op("gpsimd", lambda e: e.collective_compute(
                    "AllGather", ALU.bypass,
                    replica_groups=[list(range(NCORES))],
                    ins=[cc_in], outs=[cc_out]), cc)
        n_cc = cc.n

        gr4 = [sb(f"gr4{i}", [128, H * HD], F32, stack=pst) for i in range(1)]
        Sin2 = sb("Sin2", [128, H * HD], F32, stack=pst)
        p.op("vector", lambda e: e.memset(Sin2[:, :], 0.0))
        p.mark("vector", dv)
        p.wait("gpsimd", cc, n_cc)
        gr_free = [0]
        for r in range(NCORES):
            par4 = 0
            p.wait("gpsimd", dv, gr_free[par4])
            n_ld = p.dma("gpsimd", gr4[par4][:, :],
                         cc_out[r * 128:(r + 1) * 128, :], qg)
            src_ = Sin if r % 2 == 0 else Sin2
            dst = Sin2 if r % 2 == 0 else Sin
            p.wait("vector", qg, n_ld)
            for h in range(H):
                p.op("vector",
                     lambda e, par4=par4, r=r, h=h, s=src_, d=dst:
                     e.scalar_tensor_tensor(
                         d[:, h * HD:(h + 1) * HD],
                         gr4[par4][:, h * HD:(h + 1) * HD],
                         wcomb[:, h * NCORES + r:h * NCORES + r + 1],
                         s[:, h * HD:(h + 1) * HD], ALU.mult, ALU.add))
            gr_free[par4] = p.mark("vector", dv)
        p.op("vector", lambda e: e.tensor_copy(Sinb[:, :], Sin[:, :]))
        n_dv_sin = p.mark("vector", dv)
        if "sin_dbg" in dbg:
            p.wait("sync", dv, n_dv_sin)
            p.dma("sync", sin_dbg, Sin[:, :], qd)
            p.dma("sync", wcomb_dbg, wcomb[:, :], qd)

        n_qd_p2 = qd.n
        n_qs_p2 = qs.n
        p.flush()

    if _lvl >= 3:
        with contextlib.ExitStack() as pst:
            qdt = sb("qdt", [128, H * BL], BF16, stack=pst)
            mskt = sb("mskt", [128, 2 * H * BL], BF16, stack=pst)

            p.wait("scalar", dv, n_dv_sin)
            p.wait("sync", dv, n_dv_sin)

            p.wait("scalar", pl, n_pl0)
            for h in range(H):
                p.op("scalar", lambda e, h=h: e.activation(
                    qdt[:, h * BL:(h + 1) * BL], iotam[:, :], AF.Exp,
                    scale=nsbc[:, h:h + 1], bias=nsbc[:, h:h + 1]))
            for h in range(H):
                for c in range(2):
                    p.op("scalar", lambda e, c=c, h=h: e.activation(
                        mskt[:, (h * 2 + c) * BL:(h * 2 + c + 1) * BL],
                        idxp[:, c * BL:(c + 1) * BL], AF.Exp,
                        scale=nsbc[:, h:h + 1]))
            n_ac_tab = p.mark("scalar", ac)
            p.wait("sync", qs, n_qs_p2)
            p.wait("scalar", qs, n_qs_p2)

            with contextlib.ExitStack() as p5st:
                psO = [p5st.enter_context(
                    nc.psum_tensor(f"psO{i}", [128, 512], F32))
                    for i in range(3)]
                psQK = [p5st.enter_context(
                    nc.psum_tensor(f"psQK{i}", [128, 512], F32))
                    for i in range(3)]
                psSS = p5st.enter_context(
                    nc.psum_tensor("psSS", [128, 512], F32))
                psR = p5st.enter_context(
                    nc.psum_tensor("psR", [128, 512], F32))
                qT = [sb(f"qT{i}", [128, H * BL], BF16, stack=p5st)
                      for i in range(2)]
                kT = [sb(f"kT{i}", [128, H * BL], BF16, stack=p5st)
                      for i in range(2)]
                vtok = [sb(f"vtok{i}", [128, 2 * D], BF16, stack=p5st,
                           is_transpose_dest=True) for i in range(2)]
                ublk = sb("ublk", [128, H * BL], F32, stack=p5st)
                ob = sb("ob", [128, H * BL], F32, stack=p5st)
                sqog = sb("sqog", [128, H * BL], BF16, stack=p5st)
                qtl = [sb(f"qtl{i}", [128, H * BL], BF16, stack=p5st)
                       for i in range(2)]
                gall = [sb(f"gall{i}", [128, H * HD], BF16, stack=p5st)
                        for i in range(2)]
                bdpS = sb("bdpS", [128, H * HD], BF16, stack=p5st)
                qkm = [sb(f"qkm{i}", [128, 2 * BL], BF16, stack=p5st)
                       for i in range(4)]
                rstd5 = sb("rstd5", [1, BL], F32, stack=p5st)

                p.wait("vector", ac, n_ac_tab)

                blk_free_pe = [0, 0]
                blk_free_dv = [0, 0]
                u_free_dv = 0
                ob_free_dv = 0
                sqog_free_qd = 0
                psO_free = [(None, 0)] * 3
                psQK_free = [(None, 0)] * 3
                qtl_free = [0, 0]
                gall_free = [0, 0]
                bdpS_free = 0
                qkm_free = [0, 0, 0, 0]
                psSS_free = (None, 0)
                psR_free = (None, 0)
                last_ssmm = 0

                for b_ in range(B):
                    par = b_ % 2
                    p.wait("sync", pe, blk_free_pe[par])
                    p.wait("sync", dv, blk_free_dv[par])
                    p.dma("sync", kT[par][:, :],
                          k_img[b_ * 128:(b_ + 1) * 128, :], qd)
                    for c in range(2):
                        p.dma("sync", vtok[par][:, c * D:(c + 1) * D],
                              v_scr[:, b_ * BL + c * 128:
                                    b_ * BL + (c + 1) * 128],
                              qd, transpose=True)
                    n_tok = qd.n
                    p.wait("scalar", pe, blk_free_pe[par])
                    p.wait("scalar", dv, blk_free_dv[par])
                    n_q = p.dma("scalar", qT[par][:, :],
                                q_scr[b_ * 128:(b_ + 1) * 128, :], qs)
                    p.wait("scalar", dv, u_free_dv)
                    n_u = p.dma("scalar", ublk[:, :],
                                u_scr[b_ * 128:(b_ + 1) * 128, :], qs)

                    p.wait("vector", qs, n_q)
                    p.wait("vector", pe, qtl_free[par])
                    p.op("vector", lambda e, par=par: e.tensor_mul(
                        qtl[par][:, :], qT[par][:, :], qdt[:, :]))
                    n_qtl = p.mark("vector", dv)
                    if b_ >= 1:
                        p.wait("scalar", dv, bdpS_free)
                        for h in range(H):
                            p.op("scalar", lambda e, h=h, b_=b_:
                                 e.activation(
                                     bdpS[:, h * HD:(h + 1) * HD],
                                     Sin[:, h * HD:(h + 1) * HD], AF.Copy,
                                     scale=bdp[:, b_ * H + h:b_ * H + h + 1]))
                        n_bs = p.mark("scalar", ac)
                        p.wait("vector", ac, n_bs)
                        p.wait("vector", pe, gall_free[par])
                        p.op("vector", lambda e, par=par, b_=b_: e.tensor_add(
                            gall[par][:, :], bdpS[:, :],
                            snap[:, (b_ - 1) * H * HD:b_ * H * HD]))
                        n_gall = p.mark("vector", dv)
                        bdpS_free = n_gall
                    else:
                        n_gall = 0

                    ev_marks = []
                    for pr in range(H // 2):
                        hh = (2 * pr, 2 * pr + 1)
                        sl = ((2 * pr) % 3, (2 * pr + 1) % 3)
                        p.wait("tensor", dv, max(n_gall, n_qtl))
                        p.wait("tensor", qd, n_tok)
                        p.wait("tensor", qs, n_q)
                        for j in range(2):
                            h, hp = hh[j], sl[j]
                            if psO_free[hp][0]:
                                p.wait("tensor", psO_free[hp][0],
                                       psO_free[hp][1])
                            glhs = (gall[par] if b_ >= 1 else Sinb)
                            p.op("tensor", lambda e, hp=hp, par=par, h=h,
                                 g=glhs:
                                 e.matmul(psO[hp][:, 0:BL],
                                          g[:, h * HD:(h + 1) * HD],
                                          qtl[par][:, h * BL:(h + 1) * BL],
                                          start=True, stop=False))
                            if psQK_free[hp][0]:
                                p.wait("tensor", psQK_free[hp][0],
                                       psQK_free[hp][1])
                            for c in range(2):
                                p.op("tensor",
                                     lambda e, par=par, hp=hp, c=c, h=h:
                                     e.matmul(
                                         psQK[hp][:, c * BL:(c + 1) * BL],
                                         kT[par][:, h * BL + c * 128:
                                                 h * BL + (c + 1) * 128],
                                         qT[par][:, h * BL:(h + 1) * BL],
                                         start=True, stop=True))
                        n_qk = p.mark("tensor", pe)
                        qtl_free[par] = n_qk
                        gall_free[par] = n_qk
                        p.wait("vector", pe, n_qk)
                        for j in range(2):
                            h, hp = hh[j], sl[j]
                            qi = (pr % 2) * 2 + j
                            p.wait("vector", pe, qkm_free[qi])
                            p.op("vector", lambda e, hp=hp, qi=qi, h=h:
                                 e.tensor_mul(
                                     qkm[qi][:, :], psQK[hp][:, :],
                                     mskt[:, h * 2 * BL:(h + 1) * 2 * BL]))
                        n_qkm = p.mark("vector", dv)
                        for j in range(2):
                            psQK_free[sl[j]] = (dv, n_qkm)
                        p.wait("tensor", dv, n_qkm)
                        for j in range(2):
                            h, hp = hh[j], sl[j]
                            qi = (pr % 2) * 2 + j
                            for c in range(2):
                                p.op("tensor",
                                     lambda e, par=par, hp=hp, qi=qi, c=c, h=h:
                                     e.matmul(psO[hp][:, 0:BL],
                                              vtok[par][:, c * D + h * HD:
                                                        c * D + (h + 1) * HD],
                                              qkm[qi][:, c * BL:(c + 1) * BL],
                                              start=False, stop=(c == 1)))
                        n_diag = p.mark("tensor", pe)
                        qkm_free[(pr % 2) * 2] = n_diag
                        qkm_free[(pr % 2) * 2 + 1] = n_diag
                        p.wait("scalar", pe, n_diag)
                        if pr == 0:
                            p.wait("scalar", dv, ob_free_dv)
                        for j in range(2):
                            h, hp = hh[j], sl[j]
                            p.op("scalar", lambda e, hp=hp, h=h: e.copy(
                                ob[:, h * BL:(h + 1) * BL], psO[hp][:, 0:BL]))
                        nev = p.mark("scalar", ac)
                        for j in range(2):
                            psO_free[sl[j]] = (ac, nev)
                        ev_marks.append(nev)
                    blk_free_pe[par] = pe.n
                    blk_free_dv[par] = dv.n

                    n_all_ev = ev_marks[-1]
                    p.wait("scalar", pe, last_ssmm)
                    p.wait("scalar", qd, sqog_free_qd)
                    p.op("scalar", lambda e: e.activation(
                        sqog[:, :], ob[:, :], AF.Square))
                    n_sq = p.mark("scalar", ac)
                    p.wait("tensor", ac, n_sq)
                    if psSS_free[0]:
                        p.wait("tensor", psSS_free[0], psSS_free[1])
                    for h in range(H):
                        p.op("tensor", lambda e, h=h: e.matmul(
                            psSS[0:1, 0:BL], onP[:, :],
                            sqog[:, h * BL:(h + 1) * BL],
                            start=(h == 0), stop=(h == H - 1)))
                    n_ssmm = p.mark("tensor", pe)
                    last_ssmm = n_ssmm
                    p.wait("scalar", pe, n_ssmm)
                    p.op("scalar", lambda e: e.activation(
                        rstd5[:, :], psSS[0:1, 0:BL], AF.Ln, scale=1.0 / D,
                        bias=ceps6[0:1, :]))
                    p.op("scalar", lambda e: e.activation(
                        rstd5[:, :], rstd5[:, :], AF.Exp, scale=-0.5))
                    n_rstd = p.mark("scalar", ac)
                    psSS_free = (ac, n_rstd)
                    p.wait("tensor", ac, n_rstd)
                    if psR_free[0]:
                        p.wait("tensor", psR_free[0], psR_free[1])
                    p.op("tensor", lambda e: e.matmul(
                        psR[:, 0:BL], on1[:, :], rstd5[:, :], start=True,
                        stop=True))
                    n_rbc = p.mark("tensor", pe)
                    p.wait("vector", pe, n_rbc)
                    p.wait("vector", qs, n_u)
                    p.wait("vector", ac, n_all_ev)
                    p.op("vector", lambda e: e.tensor_mul(
                        ob[:, :], ob[:, :], ublk[:, :]))
                    for h in range(H):
                        p.op("vector", lambda e, h=h: e.tensor_mul(
                            sqog[:, h * BL:(h + 1) * BL],
                            ob[:, h * BL:(h + 1) * BL], psR[:, 0:BL]))
                    n_gat = p.mark("vector", dv)
                    psR_free = (dv, n_gat)
                    ob_free_dv = n_gat
                    u_free_dv = n_gat
                    p.wait("sync", dv, n_gat)
                    p.dma("sync", og_scr[b_ * 128:(b_ + 1) * 128, :],
                          sqog[:, :], qd)
                    sqog_free_qd = qd.n
                    p.wait("gpsimd", qd, sqog_free_qd)
                    p.dma("gpsimd", wd_bf[b_ * 1024:(b_ + 1) * 1024, :],
                          wd[b_ * 1024:(b_ + 1) * 1024, :], qg)
                    if b_ >= 4:
                        s_ = b_ - 4
                        p.dma("gpsimd", wo_bf[s_ * 512:(s_ + 1) * 512, :],
                              wout[s_ * 512:(s_ + 1) * 512, :], qg)
                n_qg_wd = qg.n
                n_qg_wo = qg.n
                n_qd_p5 = qd.n
                n_dv_p5 = dv.n
                n_ac_p5 = ac.n
                n_pe_p5 = pe.n

                p.flush()

    mid_st.close()

    if _lvl >= 4:
        with contextlib.ExitStack() as pst:
            psG = [pst.enter_context(nc.psum_tensor(f"psG{i}", [128, 512], F32))
                   for i in range(2)]
            psU = [pst.enter_context(nc.psum_tensor(f"psU{i}", [128, 512], F32))
                   for i in range(2)]
            xnt2 = sb("xnt2", [128, KC * R], BF16, stack=pst)
            wgs = [sb(f"wgs{i}", [128, KC * 512], BF16, stack=pst)
                   for i in range(2)]
            wus = [sb(f"wus{i}", [128, KC * 512], BF16, stack=pst)
                   for i in range(2)]
            gtmp = [sb(f"gtmp{i}", [128, 512], F32, stack=pst) for i in range(2)]
            utmp = [sb(f"utmp{i}", [128, 512], F32, stack=pst) for i in range(2)]
            fstg = [sb(f"fstg{i}", [128, R], BF16, stack=pst) for i in range(2)]

            p.wait("gpsimd", qd, n_qd_p5)
            p.wait("gpsimd", dv, n_dv_p5)
            p.wait("gpsimd", ac, n_ac_p5)
            p.wait("gpsimd", pe, n_pe_p5)

            p.wait("sync", qd, n_spill_xnt)
            for j in range(4):
                p.dma("sync", xnt2[:, j * 8192:(j + 1) * 8192],
                      xnt_scr[:, j * 8192:(j + 1) * 8192], qd)
            n_xnt2 = qd.n

            slab_free = [0, 0]
            fstg_free = [0, 0]
            psG_free = [(None, 0), (None, 0)]
            psU_free = [(None, 0), (None, 0)]
            gtmp_free = [0, 0]
            utmp_free = [0, 0]
            for fg in range(16):
                par = fg % 2
                p.wait("gpsimd", pe, slab_free[par])
                for k in range(KC):
                    p.dma("gpsimd", wgs[par][:, k * 512:(k + 1) * 512],
                          wg[k * 128:(k + 1) * 128, fg * 512:(fg + 1) * 512], qg)
                for k in range(KC):
                    p.dma("gpsimd", wus[par][:, k * 512:(k + 1) * 512],
                          wu[k * 128:(k + 1) * 128, fg * 512:(fg + 1) * 512], qg)
                n_slab = qg.n
                p.wait("tensor", qg, n_slab)
                p.wait("tensor", qd, n_xnt2)
                for fc in range(4):
                    fstg_par = fc % 2
                    for t in range(4):
                        sp = (fc * 4 + t) % 2
                        if psG_free[sp][0]:
                            p.wait("tensor", psG_free[sp][0], psG_free[sp][1])
                        for k in range(KC):
                            p.op("tensor",
                                 lambda e, sp=sp, par=par, fc=fc, t=t, k=k:
                                 e.matmul(psG[sp][:, :],
                                          wgs[par][:, k * 512 + fc * 128:
                                                   k * 512 + fc * 128 + 128],
                                          xnt2[:, k * R + t * 512:
                                               k * R + (t + 1) * 512],
                                          start=(k == 0), stop=(k == KC - 1)))
                        n_g = p.mark("tensor", pe)
                        if psU_free[sp][0]:
                            p.wait("tensor", psU_free[sp][0], psU_free[sp][1])
                        for k in range(KC):
                            p.op("tensor",
                                 lambda e, sp=sp, par=par, fc=fc, t=t, k=k:
                                 e.matmul(psU[sp][:, :],
                                          wus[par][:, k * 512 + fc * 128:
                                                   k * 512 + fc * 128 + 128],
                                          xnt2[:, k * R + t * 512:
                                               k * R + (t + 1) * 512],
                                          start=(k == 0), stop=(k == KC - 1)))
                        n_uu = p.mark("tensor", pe)
                        p.wait("scalar", pe, n_uu)
                        p.wait("scalar", dv, gtmp_free[sp])
                        p.op("scalar", lambda e, sp=sp, fg=fg, fc=fc: e.activation(
                            gtmp[sp][:, :], psG[sp][:, :], AF.Silu,
                            scale=sgt[:, fg * 4 + fc:fg * 4 + fc + 1]))
                        p.op("scalar", lambda e, sp=sp, fg=fg, fc=fc: e.activation(
                            utmp[sp][:, :], psU[sp][:, :], AF.Copy,
                            scale=sut[:, fg * 4 + fc:fg * 4 + fc + 1]))
                        n_ev = p.mark("scalar", ac)
                        psG_free[sp] = (ac, n_ev)
                        psU_free[sp] = (ac, n_ev)
                        p.wait("vector", ac, n_ev)
                        if t == 0:
                            p.wait("vector", qd, fstg_free[fstg_par])
                        p.op("vector", lambda e, sp=sp, fstg_par=fstg_par, t=t:
                             e.tensor_mul(fstg[fstg_par][:, t * 512:(t + 1) * 512],
                                          gtmp[sp][:, :], utmp[sp][:, :]))
                        n_f = p.mark("vector", dv)
                        gtmp_free[sp] = n_f
                        utmp_free[sp] = n_f
                    p.wait("sync", dv, n_f)
                    p.dma("sync", fuse_scr[(fg * 4 + fc) * 128:
                                           (fg * 4 + fc + 1) * 128, :],
                          fstg[fstg_par][:, :], qd)
                    fstg_free[fstg_par] = qd.n
                slab_free[par] = pe.n
            n_qd_p6a = qd.n
            n_pe_p6a = pe.n
            n_dv_p6a = dv.n
            n_ac_p6a = ac.n
            p.flush()

    if _lvl >= 5:
        with contextlib.ExitStack() as pst:
            psY = [pst.enter_context(nc.psum_tensor(f"psY{i}", [128, 512], F32))
                   for i in range(8)]
            fuT = sb("fuT", [128, 64 * 512], BF16, stack=pst)
            ogT = sb("ogT", [128, KC * 512], BF16, stack=pst)
            woT = sb("woT", [128, KC * 2048], BF16, stack=pst,
                     is_transpose_dest=True)
            wds = [sb(f"wds{i}", [128, 1024], BF16, stack=pst) for i in range(4)]
            xts = [sb(f"xts{i}", [128, 1024], F32, stack=pst) for i in range(4)]
            ystg = [sb(f"ystg{i}", [128, 512], F32, stack=pst) for i in range(2)]
            onb = sb("onb", [1, 128], F32, stack=pst)
            sd6 = sb("sd6", [1, D], F32, stack=pst)
            sdbb = sb("sdbb", [128, D], F32, stack=pst)
            rsbb = sb("rsbb", [128, D], BF16, stack=pst)

            psY_free = [(None, 0)] * 8

            p.wait("gpsimd", pe, n_pe_p6a)
            p.wait("gpsimd", dv, n_dv_p6a)
            p.wait("gpsimd", ac, n_ac_p6a)
            p.wait("gpsimd", qd, n_qd_p6a)

            p.op("gpsimd", lambda e: e.memset(onb[:, :], 1.0))
            n_pl6 = p.mark("gpsimd", pl)
            n_sd = p.dma("sync", sd6[:, :], sdr, qd)
            p.wait("tensor", pl, n_pl6)
            p.wait("tensor", qd, n_sd)
            prev = 0
            for j in range(4):
                p.wait("tensor", dv, prev)
                p.op("tensor", lambda e, j=j: e.matmul(
                    psY[0][:, :], onb[:, :], sd6[:, j * 512:(j + 1) * 512],
                    start=True, stop=True))
                nmm = p.mark("tensor", pe)
                p.wait("vector", pe, nmm)
                p.op("vector", lambda e, j=j: e.tensor_copy(
                    sdbb[:, j * 512:(j + 1) * 512], psY[0][:, :]))
                prev = p.mark("vector", dv)
            p.op("vector", lambda e: e.reciprocal(sd6[:, :], sd6[:, :]))
            n_dv_rs = p.mark("vector", dv)
            p.wait("tensor", dv, n_dv_rs)
            for j in range(4):
                p.wait("tensor", dv, prev)
                p.op("tensor", lambda e, j=j: e.matmul(
                    psY[0][:, :], onb[:, :], sd6[:, j * 512:(j + 1) * 512],
                    start=True, stop=True))
                nmm = p.mark("tensor", pe)
                p.wait("vector", pe, nmm)
                p.op("vector", lambda e, j=j: e.tensor_copy(
                    rsbb[:, j * 512:(j + 1) * 512], psY[0][:, :]))
                prev = p.mark("vector", dv)
            n_dv_bc = prev
            psY_free[0] = (dv, n_dv_bc)

            p.wait("sync", qg, n_qg_wd)
            p.wait("scalar", qg, n_qg_wd)
            p.wait("sync", qg, n_qg_wo)
            n_wo_k = []
            for k in range(KC):
                for fh2 in range(2):
                    p.dma("sync",
                          woT[:, k * 2048 + fh2 * 1024:
                              k * 2048 + (fh2 + 1) * 1024],
                          wo_bf[fh2 * 1024:(fh2 + 1) * 1024,
                                k * 128:(k + 1) * 128], qd, transpose=True)
                n_wo_k.append(qd.n)
            p.wait("vector", dv, n_dv_bc)
            woT_mark = []
            for k in range(KC):
                p.wait("vector", qd, n_wo_k[k])
                p.op("vector", lambda e, k=k: e.tensor_mul(
                    woT[:, k * 2048:(k + 1) * 2048],
                    woT[:, k * 2048:(k + 1) * 2048], rsbb[:, :]))
                woT_mark.append(p.mark("vector", dv))

            fu_free = 0
            og_free = 0
            wds_free = [(None, 0)] * 4
            xts_free = [0, 0, 0, 0]
            ystg_free = [0, 0]
            ld_done = False

            for tt in range(4):
                p.wait("sync", pe, og_free)
                p.wait("sync", qd, n_qd_p5)
                for k in range(KC):
                    for hf in range(2):
                        b0 = 2 * tt + hf
                        p.dma("sync",
                              ogT[:, k * 512 + hf * 256:k * 512 + hf * 256 + 256],
                              og_scr[b0 * 128:(b0 + 1) * 128,
                                     k * BL:(k + 1) * BL], qd)
                n_ld_og = qd.n
                p.wait("scalar", pe, fu_free)
                p.wait("scalar", qd, n_qd_p6a)
                for k in range(64):
                    p.dma("scalar", fuT[:, k * 512:(k + 1) * 512],
                          fuse_scr[k * 128:(k + 1) * 128,
                                   tt * 512:(tt + 1) * 512], qs)
                n_ld_fu = qs.n

                for fh in range(2):
                    for tsub in range(4):
                        p.wait("sync", dv, xts_free[tsub])
                        p.dma("sync", xts[tsub][:, :],
                              x[tt * 512 + tsub * 128:
                                tt * 512 + (tsub + 1) * 128,
                                fh * 1024:(fh + 1) * 1024], qd)
                    n_ld_x = qd.n
                    p.wait("tensor", qd, n_ld_og)
                    first = True
                    for k in range(KC):
                        p.wait("tensor", dv, woT_mark[k])
                        for tsub in range(4):
                            for foc in range(2):
                                sp = tsub * 2 + foc
                                if first and psY_free[sp][0]:
                                    p.wait("tensor", psY_free[sp][0],
                                           psY_free[sp][1])
                                p.op("tensor",
                                     lambda e, k=k, tsub=tsub, foc=foc, sp=sp,
                                     fr=first, fh=fh:
                                     e.matmul(psY[sp][:, :],
                                              ogT[:, k * 512 + tsub * 128:
                                                  k * 512 + tsub * 128 + 128],
                                              woT[:, k * 2048 + fh * 1024 +
                                                  foc * 512:
                                                  k * 2048 + fh * 1024 +
                                                  (foc + 1) * 512],
                                              start=fr, stop=False))
                        first = False
                    p.wait("tensor", qs, n_ld_fu)
                    for k in range(64):
                        kp = k % 4
                        ldeng, ldsem = (("sync", qd) if k % 2 == 0
                                        else ("scalar", qs))
                        if wds_free[kp][0]:
                            p.wait(ldeng, wds_free[kp][0], wds_free[kp][1])
                        n_wd_ld = p.dma(ldeng, wds[kp][:, :],
                                        wd_bf[k * 128:(k + 1) * 128,
                                              fh * 1024:(fh + 1) * 1024],
                                        ldsem)
                        p.wait("tensor", ldsem, n_wd_ld)
                        if k == 63:
                            sp_marks = {}
                        for tsub in range(4):
                            for foc in range(2):
                                sp = tsub * 2 + foc
                                p.op("tensor",
                                     lambda e, k=k, kp=kp, tsub=tsub, foc=foc,
                                     sp=sp, last=(k == 63):
                                     e.matmul(psY[sp][:, :],
                                              fuT[:, k * 512 + tsub * 128:
                                                  k * 512 + tsub * 128 + 128],
                                              wds[kp][:, foc * 512:
                                                      (foc + 1) * 512],
                                              start=False, stop=last))
                                if k == 63:
                                    sp_marks[sp] = p.mark("tensor", pe)
                        n_wd_mm = p.mark("tensor", pe)
                        wds_free[kp] = (pe, n_wd_mm)
                    n_fh_mm = p.mark("tensor", pe)
                    for tsub in range(4):
                        for foc in range(2):
                            sp = tsub * 2 + foc
                            yb = sp % 2
                            p.wait("vector", pe, n_fh_mm)
                            p.wait("vector", qd, n_ld_x)
                            p.wait("vector", qg, ystg_free[yb])
                            p.op("vector",
                                 lambda e, sp=sp, yb=yb, fh=fh, foc=foc:
                                 e.tensor_mul(ystg[yb][:, :], psY[sp][:, :],
                                              sdbb[:, fh * 1024 + foc * 512:
                                                   fh * 1024 +
                                                   (foc + 1) * 512]))
                            n_ev1 = p.mark("vector", dv)
                            psY_free[sp] = (dv, n_ev1)
                            p.op("vector",
                                 lambda e, yb=yb, tsub=tsub, foc=foc:
                                 e.tensor_add(ystg[yb][:, :], ystg[yb][:, :],
                                              xts[tsub][:, foc * 512:
                                                        (foc + 1) * 512]))
                            n_ev2 = p.mark("vector", dv)
                            if foc == 1:
                                xts_free[tsub] = n_ev2
                            p.wait("gpsimd", dv, n_ev2)
                            p.dma("gpsimd",
                                  y[tt * 512 + tsub * 128:
                                    tt * 512 + (tsub + 1) * 128,
                                    fh * 1024 + foc * 512:
                                    fh * 1024 + (foc + 1) * 512],
                                  ystg[yb][:, :], qg)
                            ystg_free[yb] = qg.n
                fu_free = pe.n
                og_free = pe.n
            p.wait("sync", qd, qd.n)
            p.wait("sync", qg, qg.n)
            p.flush()

    if _lvl < 5:
        p.wait("sync", qd, qd.n)
        p.wait("sync", qg, qg.n)
        p.wait("sync", qs, qs.n)
        p.wait("sync", cc, cc.n)
        p.wait("vector", dv, dv.n)
        p.wait("scalar", ac, ac.n)
        p.wait("tensor", pe, pe.n)
        p.wait("gpsimd", pl, pl.n)
        p.flush()
    st.close()
    return nc


_NC_CACHE = {}


def _get_nc():
    if "nc" not in _NC_CACHE:
        _NC_CACHE["nc"] = build_nc()
    return _NC_CACHE["nc"]


def make_in_maps(L0_attn_x, L0_attn_slope_rate, L0_attn_W_qkvu,
                 L0_attn_W_out, L0_ffn_w_gate, L0_ffn_w_up, L0_ffn_w_down,
                 L0_ffn_s_gate, L0_ffn_s_up, L0_ffn_s_down):
    xf = np.ascontiguousarray(np.asarray(L0_attn_x).reshape(NCORES * R, D),
                              dtype=np.float32)
    slope = np.ascontiguousarray(np.asarray(L0_attn_slope_rate).reshape(H),
                                 dtype=np.float32)
    wq = np.ascontiguousarray(np.asarray(L0_attn_W_qkvu), dtype=np.float32)
    wo = np.ascontiguousarray(np.asarray(L0_attn_W_out), dtype=np.float32)
    wg_ = np.ascontiguousarray(np.asarray(L0_ffn_w_gate), dtype=np.int8)
    wu_ = np.ascontiguousarray(np.asarray(L0_ffn_w_up), dtype=np.int8)
    wd_ = np.ascontiguousarray(np.asarray(L0_ffn_w_down), dtype=np.int8)
    sg_ = np.ascontiguousarray(np.asarray(L0_ffn_s_gate).reshape(FI // 128,
                                                                 128),
                               dtype=np.float32)
    su_ = np.ascontiguousarray(np.asarray(L0_ffn_s_up).reshape(FI // 128,
                                                               128),
                               dtype=np.float32)
    sd_ = np.ascontiguousarray(np.asarray(L0_ffn_s_down).reshape(1, D),
                               dtype=np.float32)

    in_maps = []
    for c in range(NCORES):
        pm = np.array([[1.0 if r < c else 0.0 for r in range(NCORES)]],
                      dtype=np.float32)
        pd = np.array([[float(c - 1 - r) if r < c else 0.0
                        for r in range(NCORES)]], dtype=np.float32)
        in_maps.append({
            "x": np.ascontiguousarray(xf[c * R:(c + 1) * R]),
            "slope": slope, "wqkvu": wq, "wout": wo,
            "wg": wg_, "wu": wu_, "wd": wd_,
            "sg": sg_, "su": su_, "sdr": sd_,
            "pmask": pm, "pdist": pd,
        })
    return in_maps


def kernel(**inputs):
    in_maps = make_in_maps(**inputs)
    nc = _get_nc()
    res = run_bass_kernel_spmd(nc, in_maps, core_ids=list(range(NCORES)))
    out = np.concatenate([np.asarray(r["y"]) for r in res.results], axis=0)
    return out.reshape(1, NCORES * R, D).astype(np.float32)
